# version 10
# baseline (speedup 1.0000x reference)
import numpy as np
from contextlib import ExitStack

import concourse.bass as bass
import concourse.mybir as mybir
from concourse.bass_utils import run_bass_kernel_spmd

F32 = mybir.dt.float32
BF16 = mybir.dt.bfloat16
I8 = mybir.dt.int8
AF = mybir.ActivationFunctionType
ALU = mybir.AluOpType
AX = mybir.AxisListType

ENGINES = ("tensor", "vector", "scalar", "gpsimd", "sync")


class Cfg:
    def __init__(self, T_main=2048, W=128, NE=64, I_exp=704, SI=1408,
                 NT=512, GROUP=5, n_cores=8, debug=(), phases="123"):
        self.phases = phases
        self.T_main = T_main
        self.W = W
        self.T = T_main + W
        self.NE = NE
        self.I_exp = I_exp
        self.SI = SI
        self.NI = NE * I_exp + SI
        assert self.NI % 128 == 0
        self.IC = self.NI // 128
        self.NT = NT
        self.GROUP = GROUP
        self.D = 1024
        self.DC = 8
        self.H = 16
        self.HD = 64
        self.n_cores = n_cores
        self.debug = tuple(debug)
        assert T_main % NT == 0 and W % 128 == 0
        self.t_tiles_1 = ([(0, W)] if W else []) + \
            [(W + i * NT, NT) for i in range(T_main // NT)]
        self.t_tiles_3 = [(W + i * NT, NT) for i in range(T_main // NT)]
        self.n_tc = self.T // 128
        self.groups = []
        c = 0
        while c < self.IC:
            n = min(GROUP, self.IC - c)
            self.groups.append((c, n))
            c += n


class Sem:
    __slots__ = ("name", "h", "count")

    def __init__(self, name, h):
        self.name, self.h, self.count = name, h, 0


GP_COMPUTE = {"memset", "iota", "affine_select", "tensor_copy",
              "tensor_tensor", "tensor_scalar", "tensor_reduce"}


class Prog:

    def __init__(self, nc):
        self.nc = nc
        self.q = {e: [] for e in ENGINES}
        self.last_wait = {e: {} for e in ENGINES}
        self.auto = {}
        self.last_comp = {e: None for e in ENGINES}

    def set_auto(self, engine, sem):
        self.auto[engine] = sem

    def op(self, engine, method, *args, inc=None, **kwargs):
        chain = (engine in ("vector", "scalar")
                 or (engine == "gpsimd" and method in GP_COMPUTE))
        if chain and inc is None and engine in self.auto:
            inc = (self.auto[engine], 1)
        if chain:
            self.wait(engine, self.last_comp[engine])
        tok = None
        if inc is not None:
            sem, amt = inc
            sem.count += amt
            tok = (sem, sem.count)

        def fn(eng, method=method, args=args, kwargs=kwargs, inc=inc):
            inst = getattr(eng, method)(*args, **kwargs)
            if inc is not None:
                inst.then_inc(inc[0].h, inc[1])
        self.q[engine].append(fn)
        if chain:
            self.last_comp[engine] = tok
        return tok

    def wait(self, engine, tok):
        if tok is None:
            return
        sem, val = tok
        if self.last_wait[engine].get(sem.name, 0) >= val:
            return
        self.last_wait[engine][sem.name] = val

        def fn(eng, sem=sem, val=val):
            eng.wait_ge(sem.h, val)
        self.q[engine].append(fn)

    def waits(self, engine, *toks):
        for t in toks:
            self.wait(engine, t)

    def flush(self, block):
        for e in ENGINES:
            ops = self.q[e]
            if not ops:
                continue
            self.q[e] = []

            def runner(eng, ops=ops):
                for f in ops:
                    f(eng)
            getattr(block, e)(runner)


def ts(i, n):
    return slice(i * n, (i + 1) * n)



def build_program(cfg: Cfg):
    nc = bass.Bass("TRN2", target_bir_lowering=False)
    D, DC, NT, W, T = cfg.D, cfg.DC, cfg.NT, cfg.W, cfg.T
    NI = cfg.NI
    NEx = cfg.NE + 1
    H, HD = cfg.H, cfg.HD
    LNEPS = 1e-5

    ap = {}

    def din(name, shape, dt=F32):
        ap[name] = nc.dram_tensor(name, shape, dt, kind="ExternalInput").ap()

    din("xT", [D, T])
    din("gateT", [D, NI])
    din("upT", [D, NI])
    din("downT", [NI, D])
    din("m1T", [NEx, NI], BF16)
    for nm in ("wkT", "wvT", "wrT", "wgT", "woT"):
        din(nm, [D, D])
    din("gwT", [D, cfg.NE])
    din("td_s", [128, 8])
    din("tf_s", [128, 8])
    din("tm_s", [128, 8])
    din("td_row", [1, D])
    din("gb_row", [1, cfg.NE])
    ap["y"] = nc.dram_tensor("y", [D, cfg.T_main], F32, kind="ExternalOutput").ap()

    dbg_outs = {}

    def dbg(name, shape, dt=F32):
        if name in cfg.debug:
            if name not in dbg_outs:
                dbg_outs[name] = nc.dram_tensor("dbg_" + name, shape, dt,
                                                kind="ExternalOutput").ap()
            return dbg_outs[name]
        return None

    P = Prog(nc)

    with ExitStack() as g:
        sbuf = lambda n, s, dt=F32: g.enter_context(nc.sbuf_tensor(n, s, dt))
        sem = lambda n: Sem(n, g.enter_context(nc.semaphore(n)))

        x_sb = sbuf("x_sb", [128, DC * T])
        ident_bf = sbuf("ident_bf", [128, 128], BF16)
        tri_u8 = sbuf("tri_u8", [128, 128], I8)
        zeros_bf = sbuf("zeros_bf", [128, 128], BF16)
        ones_col = sbuf("ones_col", [128, 1])
        ones_row = sbuf("ones_row", [1, 128])
        td_s = sbuf("td_s_sb", [128, 8])
        tf_s = sbuf("tf_s_sb", [128, 8])
        tm_s = sbuf("tm_s_sb", [128, 8])
        td_row = sbuf("td_row_sb", [1, D])
        gb_sb = sbuf("gb_sb", [1, cfg.NE])
        gbrep = sbuf("gbrep", [128, cfg.NE])
        gwT_sb = sbuf("gwT_sb", [128, DC * cfg.NE], BF16)
        gscr = sbuf("gscr", [128, 128])

        s_ld = sem("s_ld")
        s_wg = [sem("s_wg0"), sem("s_wg1")]
        s_wu = [sem("s_wu0"), sem("s_wu1")]
        s_wd = [sem("s_wd0"), sem("s_wd1")]
        s_m = [sem("s_m0"), sem("s_m1")]
        s_wa = sem("s_wa")
        s_pe = sem("s_pe")
        s_act = sem("s_act")
        s_dve = sem("s_dve")
        s_gp = sem("s_gp")
        s_st = sem("s_st")
        P.set_auto("vector", s_dve)
        P.set_auto("scalar", s_act)
        P.set_auto("gpsimd", s_gp)

        block = g.enter_context(nc.Block())

        P.op("sync", "dma_start",
             out=x_sb[:].rearrange("p (c t) -> p c t", t=T),
             in_=ap["xT"].rearrange("(c p) t -> p c t", p=128),
             inc=(s_ld, 16))
        for name, tile in (("td_s", td_s), ("tf_s", tf_s), ("tm_s", tm_s),
                           ("td_row", td_row), ("gb_row", gb_sb)):
            P.op("sync", "dma_start", out=tile[:], in_=ap[name][:, :],
                 inc=(s_ld, 16))
        t_small = (s_ld, s_ld.count)
        t_x = t_small
        t_gwT = P.op("gpsimd", "dma_start",
                     out=gwT_sb[:].rearrange("p (c e) -> p c e", e=cfg.NE),
                     in_=ap["gwT"].rearrange("(c p) e -> p c e", p=128),
                     inc=(s_wa, 16))

        P.op("gpsimd", "memset", ident_bf[:], 0.0)
        P.op("gpsimd", "affine_select", out=ident_bf[:], in_=ident_bf[:],
             compare_op=ALU.not_equal, fill=1.0, base=0,
             pattern=[[-1, 128]], channel_multiplier=1)
        P.op("gpsimd", "memset", zeros_bf[:], 0.0)
        P.op("gpsimd", "memset", gscr[:], 0.0)
        t_trif = P.op("gpsimd", "affine_select", out=gscr[:], in_=gscr[:],
                      compare_op=ALU.is_gt, fill=1.0, base=0,
                      pattern=[[-1, 128]], channel_multiplier=1)
        P.op("gpsimd", "memset", ones_col[:], 1.0)
        t_gp_const = P.op("gpsimd", "memset", ones_row[:], 1.0, inc=(s_gp, 1))
        P.wait("vector", t_gp_const)
        t_tri = P.op("vector", "tensor_copy", tri_u8[:], gscr[:])

        with nc.psum_tensor("ps_init", [128, 512], F32) as ps_init:
            P.waits("tensor", t_small, t_gp_const)
            t_mm = P.op("tensor", "matmul", ps_init[:, 0:cfg.NE], ones_row[:],
                        gb_sb[:], start=True, stop=True, inc=(s_pe, 1))
            P.wait("scalar", t_mm)
            t_gbrep = P.op("scalar", "copy", gbrep[:], ps_init[:, 0:cfg.NE],
                           inc=(s_act, 1))
            P.wait("tensor", t_gbrep)

        def emit_ln_apply(scope, x_cols, out_tile, out_w, out_off, with_tm):
            psR, psAB = scope["psR"], scope["psAB"]
            x2, rows, tscr = scope["x2"], scope["rows"], scope["tscr"]
            rs = scope.get("rstride", 512)
            xw = scope.get("x2w", 512)
            t0, tlen = x_cols
            last = None
            nblk = (tlen + NT - 1) // NT
            tm2_toks = {}
            for bi in range(nblk):
                b0 = t0 + bi * NT
                blen = min(NT, tlen - bi * NT)
                ob0 = out_off + bi * NT
                for dc in range(DC):
                    xs = x_sb[:, dc * T + b0: dc * T + b0 + blen]
                    par = dc % 2
                    P.wait("vector", tm2_toks.get((bi, dc - 2)))
                    tw = P.op("vector", "tensor_tensor",
                              x2[:, par * xw:par * xw + blen], xs, xs,
                              ALU.mult, inc=(s_dve, 1))
                    if scope.get("t_pe_pre") is not None and bi == 0 and dc == 0:
                        P.wait("tensor", scope["t_pe_pre"])
                    P.op("tensor", "matmul", psR[0:1, 0:blen], ones_col[:], xs,
                         start=(dc == 0), stop=(dc == DC - 1))
                    P.wait("tensor", tw)
                    tm2_toks[(bi, dc)] = P.op(
                        "tensor", "matmul", psR[0:1, 512:512 + blen], ones_col[:],
                        x2[:, par * xw:par * xw + blen],
                        start=(dc == 0), stop=(dc == DC - 1), inc=(s_pe, 1))
                t_stats = tm2_toks[(bi, DC - 1)]
                P.wait("vector", t_stats)
                mean = rows[0:1, 0:blen]
                msq = rows[0:1, rs:rs + blen]
                Arow = rows[0:1, 2 * rs:2 * rs + blen]
                Brow = rows[0:1, 3 * rs:3 * rs + blen]
                P.op("vector", "tensor_scalar", mean, psR[0:1, 0:blen],
                     1.0 / D, None, ALU.mult)
                P.op("vector", "tensor_scalar", msq, psR[0:1, 512:512 + blen],
                     1.0 / D, None, ALU.mult)
                P.op("vector", "tensor_tensor", Arow, mean, mean, ALU.mult)
                P.op("vector", "tensor_tensor", Arow, msq, Arow, ALU.subtract)
                t_v = P.op("vector", "tensor_scalar", Arow, Arow, LNEPS, None,
                           ALU.add, inc=(s_dve, 1))
                P.wait("scalar", t_v)
                P.op("scalar", "activation", out=Arow, in_=Arow, func=AF.Ln)
                t_a = P.op("scalar", "activation", out=Arow, in_=Arow,
                           func=AF.Exp, scale=-0.5, inc=(s_act, 1))
                P.wait("vector", t_a)
                t_b = P.op("vector", "tensor_tensor", Brow, mean, Arow,
                           ALU.mult, inc=(s_dve, 1))
                P.wait("tensor", t_b)
                ta = P.op("tensor", "matmul", psAB[:, 0:blen], ones_row[:],
                          Arow, start=True, stop=True)
                tb = P.op("tensor", "matmul", psAB[:, 512:512 + blen],
                          ones_row[:], Brow, start=True, stop=True,
                          inc=(s_pe, 1))
                P.wait("vector", tb)
                for dc in range(DC):
                    xs = x_sb[:, dc * T + b0: dc * T + b0 + blen]
                    P.op("vector", "tensor_tensor", tscr[:, 0:blen], xs,
                         psAB[:, 0:blen], ALU.mult)
                    if with_tm:
                        P.op("vector", "tensor_tensor", tscr[:, 0:blen],
                             tscr[:, 0:blen], psAB[:, 512:512 + blen],
                             ALU.subtract)
                        last = P.op("vector", "tensor_scalar",
                                    out_tile[:, dc * out_w + ob0:
                                             dc * out_w + ob0 + blen],
                                    tscr[:, 0:blen], tm_s[:, dc:dc + 1], None,
                                    ALU.mult, inc=(s_dve, 1))
                    else:
                        last = P.op("vector", "tensor_tensor",
                                    out_tile[:, dc * out_w + ob0:
                                             dc * out_w + ob0 + blen],
                                    tscr[:, 0:blen], psAB[:, 512:512 + blen],
                                    ALU.subtract, inc=(s_dve, 1))
                P.wait("tensor", last)
            return last

        def emit_ffn(call, t_prev):
            t_tiles = cfg.t_tiles_1 if call == 1 else cfg.t_tiles_3
            TW = T
            with ExitStack() as fs:
                fsb = lambda n, s, dt=F32: fs.enter_context(
                    nc.sbuf_tensor(f"{n}_c{call}", s, dt))
                xln = fsb("xln", [128, DC * TW], BF16)
                rw_fm = fsb("rw_fm", [128, TW], BF16)
                GW = cfg.GROUP * 128
                wg_b = fsb("wg_b", [128, 2 * DC * GW], BF16)
                wu_b = fsb("wu_b", [128, 2 * DC * GW], BF16)
                wd_b = fsb("wd_b", [128, 2 * cfg.GROUP * D], BF16)
                m1_b = fsb("m1_b", [NEx, 2 * GW], BF16)
                h_sb = fsb("h_sb", [128, 2 * cfg.GROUP * NT], BF16)
                sS = fsb("sS", [128, 2 * NT], BF16)
                t1 = fsb("t1", [128, 2 * NT])


                with ExitStack() as lns:
                    lpsR = lns.enter_context(
                        nc.psum_tensor(f"lnpsR_{call}", [1, 1024], F32))
                    lpsAB = lns.enter_context(
                        nc.psum_tensor(f"lnpsAB_{call}", [128, 1024], F32))
                    lx2 = lns.enter_context(
                        nc.sbuf_tensor(f"lx2_{call}", [128, 1024], F32))
                    lrows = lns.enter_context(
                        nc.sbuf_tensor(f"lrows_{call}", [1, 2048], F32))
                    ltscr = lns.enter_context(
                        nc.sbuf_tensor(f"ltscr_{call}", [128, 512], F32))
                    scope = {"psR": lpsR, "psAB": lpsAB, "x2": lx2,
                             "rows": lrows, "tscr": ltscr, "t_pe_pre": t_prev}
                    if call == 1:
                        P.wait("vector", t_x)
                        t_ln = emit_ln_apply(scope, (0, T), xln, TW, 0, False)
                    else:
                        t_ln = emit_ln_apply(scope, (W, cfg.T_main), xln, TW, W,
                                             False)

                    psRt = lns.enter_context(
                        nc.psum_tensor(f"rpsR_{call}", [128, 1024], F32))
                    psTp = lns.enter_context(
                        nc.psum_tensor(f"rpsT_{call}", [128, 128], BF16))
                    sc = lns.enter_context(
                        nc.sbuf_tensor(f"rsc_{call}", [128, 2 * 64], F32))
                    s4c = lns.enter_context(
                        nc.sbuf_tensor(f"rs4c_{call}", [128, 64], F32))
                    gsv = lns.enter_context(
                        nc.sbuf_tensor(f"rgs_{call}", [128, 8], F32))
                    m8 = lns.enter_context(
                        nc.sbuf_tensor(f"rm8_{call}", [128, 8], F32))
                    msk = lns.enter_context(
                        nc.sbuf_tensor(f"rmsk_{call}", [128, 64], F32))
                    gm = lns.enter_context(
                        nc.sbuf_tensor(f"rgm_{call}", [128, 8], F32))
                    rww = lns.enter_context(
                        nc.sbuf_tensor(f"rww_{call}", [128, 64], F32))
                    den = lns.enter_context(
                        nc.sbuf_tensor(f"rden_{call}", [128, 1], F32))
                    rwtm = lns.enter_context(
                        nc.sbuf_tensor(f"rwtm_{call}", [128, 64], BF16))

                    tc_range = list(range(cfg.n_tc)) if call == 1 else \
                        list(range(1, cfg.n_tc))
                    P.op("vector", "memset", rw_fm[64:65, 0:TW], 0.5)
                    P.wait("tensor", t_ln)
                    P.waits("tensor", t_gwT)
                    sc_tok = {0: None, 1: None}
                    prev_evac = None
                    prev_dve = None
                    for tc in tc_range:
                        par = tc % 2
                        P.wait("tensor", sc_tok[par])
                        for dc in range(DC):
                            tmm = P.op("tensor", "matmul",
                                       psRt[:, par * 512:par * 512 + cfg.NE],
                                       xln[:, dc * TW + tc * 128:
                                           dc * TW + (tc + 1) * 128],
                                       gwT_sb[:, ts(dc, cfg.NE)],
                                       start=(dc == 0), stop=(dc == DC - 1),
                                       inc=(s_pe, 1) if dc == DC - 1 else None)
                        P.wait("scalar", tmm)
                        P.wait("scalar", prev_dve)
                        sc_tok[par] = P.op("scalar", "activation",
                                           out=sc[:, ts(par, 64)],
                                           in_=psRt[:, par * 512:par * 512 + cfg.NE],
                                           func=AF.Sigmoid, inc=(s_act, 1))
                        P.wait("vector", sc_tok[par])
                        scv = sc[:, ts(par, 64)]
                        P.op("vector", "tensor_tensor", s4c[:], scv, gbrep[:],
                             ALU.add)
                        for gi in range(8):
                            P.op("vector", "max", out=m8[:], in_=s4c[:, ts(gi, 8)])
                            P.op("vector", "tensor_tensor", gsv[:, gi:gi + 1],
                                 m8[:, 0:1], m8[:, 1:2], ALU.add)
                        P.op("vector", "max", out=m8[:], in_=gsv[:])
                        P.op("vector", "tensor_scalar", gm[:], gsv[:],
                             m8[:, 3:4], None, ALU.is_ge)
                        for gi in range(8):
                            P.op("vector", "tensor_scalar", msk[:, ts(gi, 8)],
                                 s4c[:, ts(gi, 8)], gm[:, gi:gi + 1], None,
                                 ALU.mult)
                        P.op("vector", "max", out=m8[:], in_=msk[:])
                        P.op("vector", "tensor_scalar", msk[:], msk[:],
                             m8[:, 7:8], None, ALU.is_ge)
                        P.op("vector", "tensor_tensor", rww[:], scv, msk[:],
                             ALU.mult)
                        P.op("vector", "tensor_reduce", den[:], rww[:],
                             axis=AX.X, op=ALU.add)
                        P.op("vector", "tensor_scalar", den[:], den[:], 1e-20,
                             None, ALU.add)
                        P.op("vector", "reciprocal", den[:], den[:])
                        P.op("vector", "tensor_scalar", den[:], den[:], 1.25,
                             None, ALU.mult)
                        t_rwtm = P.op("vector", "tensor_scalar", rwtm[:],
                                      rww[:], den[:], None, ALU.mult,
                                      inc=(s_dve, 1))
                        prev_dve = t_rwtm
                        P.wait("tensor", t_rwtm)
                        P.wait("tensor", prev_evac)
                        t_tp = P.op("tensor", "transpose", psTp[0:64, :],
                                    rwtm[:], ident_bf[:], inc=(s_pe, 1))
                        P.wait("vector", t_tp)
                        prev_evac = P.op("vector", "tensor_copy",
                                         rw_fm[0:64, tc * 128:(tc + 1) * 128],
                                         psTp[0:64, :], inc=(s_dve, 1))
                    t_rw_ready = prev_evac
                    if call == 1:
                        for nm_, tile_, shape_, dt_ in (
                                ("rt_sc", sc, [128, 128], F32),
                                ("rt_s4c", s4c, [128, 64], F32),
                                ("rt_msk", msk, [128, 64], F32),
                                ("rt_rww", rww, [128, 64], F32),
                                ("rt_den", den, [128, 1], F32),
                                ("rt_m8", m8, [128, 8], F32),
                                ("rt_gm", gm, [128, 8], F32),
                                ("rt_gsv", gsv, [128, 8], F32),
                                ("rt_rwtm", rwtm, [128, 64], BF16),
                                ("rt_gbrep", gbrep, [128, 64], F32)):
                            d_ = dbg(nm_, shape_, dt_)
                            if d_ is not None:
                                P.wait("sync", t_rw_ready)
                                P.op("sync", "dma_start", out=d_[:, :],
                                     in_=tile_[:], inc=(s_st, 16))
                    d = dbg(f"rw_{call}", [128, TW], BF16)
                    if d is not None:
                        P.wait("sync", t_rw_ready)
                        P.op("sync", "dma_start", out=d[:, :], in_=rw_fm[:],
                             inc=(s_st, 16))
                    d = dbg(f"xln_{call}", [128, DC * TW], BF16)
                    if d is not None:
                        P.wait("sync", t_ln)
                        P.op("sync", "dma_start", out=d[:, :], in_=xln[:],
                             inc=(s_st, 16))

                psA = fs.enter_context(nc.psum_tensor(f"psA_{call}", [128, 1024], F32))
                psB = fs.enter_context(nc.psum_tensor(f"psB_{call}", [128, 1024], F32))
                psC = fs.enter_context(nc.psum_tensor(f"psC_{call}", [128, 1024], F32))
                psD = fs.enter_context(nc.psum_tensor(f"psD_{call}", [128, 1024], F32))

                gateT_v = ap["gateT"].rearrange("(c p) i -> p c i", p=128)
                upT_v = ap["upT"].rearrange("(c p) i -> p c i", p=128)
                downT_v = ap["downT"].rearrange("(g p) d -> p g d", p=128)

                w_tok = {}
                free_tok = {0: t_prev, 1: t_prev}

                def emit_group_piece(gi, piece, gate_tok):
                    c0, ncg = cfg.groups[gi]
                    par = gi % 2
                    i0, wlen = c0 * 128, ncg * 128
                    P.wait("gpsimd", gate_tok)
                    if piece == 0:
                        P.wait("gpsimd", free_tok[par])
                        P.op("gpsimd", "dma_start",
                             out=wg_b[:, par * DC * GW: (par + 1) * DC * GW]
                             .rearrange("p (c i) -> p c i", i=GW)[:, :, 0:wlen],
                             in_=gateT_v[:, :, i0:i0 + wlen],
                             inc=(s_wg[par], 16))
                        P.wait("sync", free_tok[par])
                        P.wait("sync", gate_tok)
                        P.op("sync", "dma_start",
                             out=m1_b[:, par * GW: par * GW + wlen],
                             in_=ap["m1T"][:, i0:i0 + wlen],
                             inc=(s_m[par], 16))
                    elif piece == 1:
                        P.op("gpsimd", "dma_start",
                             out=wu_b[:, par * DC * GW: (par + 1) * DC * GW]
                             .rearrange("p (c i) -> p c i", i=GW)[:, :, 0:wlen],
                             in_=upT_v[:, :, i0:i0 + wlen],
                             inc=(s_wu[par], 16))
                    else:
                        P.op("gpsimd", "dma_start",
                             out=wd_b[:, par * cfg.GROUP * D:
                                      par * cfg.GROUP * D + ncg * D]
                             .rearrange("p (c d) -> p c d", d=D),
                             in_=downT_v[:, c0:c0 + ncg, :],
                             inc=(s_wd[par], 16))
                    w_tok[gi] = tuple((s_, s_.count) for s_ in
                                      (s_wg[par], s_wu[par], s_wd[par],
                                       s_m[par]))

                def issue_group_dma(gi):
                    for pc in range(3):
                        emit_group_piece(gi, pc, None)

                issue_group_dma(0)
                if len(cfg.groups) > 1:
                    issue_group_dma(1)
                pending = []

                silu_tok = [None, None]
                h_tok = [None, None]
                flush_tok = [None, None]
                psC_last = [None, None]
                sS_last = [None, None]
                h_guard = [None, None]
                t_first_wait_done = False
                t_fl = None

                for gi, (c0, ncg) in enumerate(cfg.groups):
                    par = gi % 2
                    gbase_g = par * DC * GW
                    gbase_d = par * cfg.GROUP * D
                    gbase_m = par * GW
                    gbase_h = par * cfg.GROUP * NT
                    P.waits("tensor", *w_tok[gi])
                    for (tt0, tlen) in t_tiles:
                        for ci in range(ncg):
                            cpar = ci % 2
                            if not t_first_wait_done:
                                P.wait("tensor", t_rw_ready)
                                t_first_wait_done = True
                            P.wait("tensor", psC_last[cpar])
                            t_rwmm = P.op("tensor", "matmul",
                                          psC[:, cpar * 512: cpar * 512 + tlen],
                                          m1_b[:, gbase_m + ci * 128:
                                               gbase_m + (ci + 1) * 128],
                                          rw_fm[0:NEx, tt0:tt0 + tlen],
                                          start=True, stop=True, inc=(s_pe, 1))
                            P.wait("tensor", silu_tok[cpar])
                            for dc in range(DC):
                                t_gmm = P.op("tensor", "matmul",
                                             psA[:, cpar * 512: cpar * 512 + tlen],
                                             wg_b[:, gbase_g + dc * GW + ci * 128:
                                                  gbase_g + dc * GW + (ci + 1) * 128],
                                             xln[:, dc * TW + tt0:
                                                 dc * TW + tt0 + tlen],
                                             start=(dc == 0), stop=(dc == DC - 1),
                                             inc=(s_pe, 1) if dc == DC - 1 else None)
                            P.wait("tensor", h_tok[cpar])
                            for dc in range(DC):
                                t_umm = P.op("tensor", "matmul",
                                             psB[:, cpar * 512: cpar * 512 + tlen],
                                             wu_b[:, gbase_g + dc * GW + ci * 128:
                                                  gbase_g + dc * GW + (ci + 1) * 128],
                                             xln[:, dc * TW + tt0:
                                                 dc * TW + tt0 + tlen],
                                             start=(dc == 0), stop=(dc == DC - 1),
                                             inc=(s_pe, 1) if dc == DC - 1 else None)
                            P.wait("scalar", t_gmm)
                            P.wait("scalar", sS_last[cpar])
                            t_silu = P.op("scalar", "activation",
                                          out=sS[:, cpar * NT: cpar * NT + tlen],
                                          in_=psA[:, cpar * 512: cpar * 512 + tlen],
                                          func=AF.Silu, inc=(s_act, 1))
                            silu_tok[cpar] = t_silu
                            P.waits("vector", t_silu, t_rwmm)
                            t_t1 = P.op("vector", "tensor_tensor",
                                        t1[:, cpar * NT: cpar * NT + tlen],
                                        sS[:, cpar * NT: cpar * NT + tlen],
                                        psC[:, cpar * 512: cpar * 512 + tlen],
                                        ALU.mult, inc=(s_dve, 1))
                            psC_last[cpar] = t_t1
                            sS_last[cpar] = t_t1
                            P.wait("vector", t_umm)
                            P.wait("vector", h_guard[par])
                            t_hw = P.op("vector", "tensor_tensor",
                                        h_sb[:, gbase_h + ci * NT:
                                             gbase_h + ci * NT + tlen],
                                        t1[:, cpar * NT: cpar * NT + tlen],
                                        psB[:, cpar * 512: cpar * 512 + tlen],
                                        ALU.mult, inc=(s_dve, 1))
                            h_tok[cpar] = t_hw
                            last_h = t_hw
                        P.wait("tensor", last_h)
                        for mm in range(8):
                            dpar = mm % 2
                            P.wait("tensor", flush_tok[dpar])
                            for ci in range(ncg):
                                tmm = P.op("tensor", "matmul",
                                           psD[:, dpar * 512: dpar * 512 + tlen],
                                           wd_b[:, gbase_d + ci * D + mm * 128:
                                                gbase_d + ci * D + (mm + 1) * 128],
                                           h_sb[:, gbase_h + ci * NT:
                                                gbase_h + ci * NT + tlen],
                                           start=(ci == 0), stop=(ci == ncg - 1),
                                           inc=(s_pe, 1)
                                           if ci == ncg - 1 else None)
                            P.wait("vector", tmm)
                            xs = x_sb[:, mm * T + tt0: mm * T + tt0 + tlen]
                            t_fl = P.op("vector", "tensor_tensor", xs, xs,
                                        psD[:, dpar * 512: dpar * 512 + tlen],
                                        ALU.add, inc=(s_dve, 1))
                            flush_tok[dpar] = t_fl
                        h_guard[par] = tmm
                        if pending:
                            gi2, pc = pending.pop(0)
                            emit_group_piece(gi2, pc, tmm)
                    while pending:
                        gi2, pc = pending.pop(0)
                        emit_group_piece(gi2, pc, h_guard[par])
                    if gi + 2 < len(cfg.groups):
                        free_tok[par] = h_guard[par]
                        pending = [(gi + 2, 0), (gi + 2, 1), (gi + 2, 2)]
                t_done = t_fl
                if call == 1:
                    for nm_, tile_, dt_ in (("ff_wg", wg_b, BF16),
                                            ("ff_wu", wu_b, BF16),
                                            ("ff_wd", wd_b, BF16),
                                            ("ff_m1", m1_b, BF16),
                                            ("ff_h", h_sb, BF16),
                                            ("ff_sS", sS, BF16),
                                            ("ff_t1", t1, F32)):
                        d_ = dbg(nm_, list(tile_[:].shape), dt_)
                        if d_ is not None:
                            P.wait("sync", t_done)
                            P.op("sync", "dma_start", out=d_[:, :],
                                 in_=tile_[:], inc=(s_st, 16))
                d = dbg(f"x_after_{call}", [128, DC * T])
                if d is not None:
                    P.wait("sync", t_done)
                    P.op("sync", "dma_start", out=d[:, :], in_=x_sb[:],
                         inc=(s_st, 16))
                return t_done

        def emit_attn(t_prev):
            with ExitStack() as asx:
                asb = lambda n, s, dt=F32: asx.enter_context(
                    nc.sbuf_tensor(n, s, dt))
                wk_sb = asb("wk_sb", [128, DC * D], BF16)
                wv_sb = asb("wv_sb", [128, DC * D], BF16)
                wr_sb = asb("wr_sb", [128, DC * D], BF16)
                wgg_sb = asb("wgg_sb", [128, DC * D], BF16)
                wo_sb = asb("wo_sb", [128, DC * D], BF16)
                PThL = asb("PThL", [128, DC * 128], BF16)
                PTtL = asb("PTtL", [128, DC * 128], BF16)
                PTtB = asb("PTtB", [128, DC * 64], BF16)
                PTq = asb("PTq", [128, DC * 128], BF16)
                tf_full = asb("tf_full", [128, DC * 128], BF16)
                REVP = asb("REVP", [128, D])
                WCf = asb("WCf", [128, 512])
                IL = asb("IL", [128, 64])
                IB96 = asb("IB96", [128, 64])
                Iq = asb("Iq", [128, 128])
                revrow = asb("revrow", [1, 128])
                e1 = asb("e1", [128, 8])
                ptscr = asb("ptscr", [128, 128])
                S_true = asb("S_true", [128, 512])
                S_bf = asb("S_bf", [128, 512], BF16)
                xx = asb("xx", [128, DC * 128], BF16)
                Kfm = asb("Kfm", [128, DC * 128], BF16)
                Vfm = asb("Vfm", [128, DC * 128], BF16)
                Rfm = asb("Rfm", [128, DC * 128], BF16)
                Gfm = asb("Gfm", [128, DC * 128], BF16)
                Ktm = asb("Ktm", [128, D], BF16)
                Vtm = asb("Vtm", [128, D], BF16)
                k2 = asb("k2", [128, D], BF16)
                khatL = asb("khatL", [128, DC * 128], BF16)
                ktilL = asb("ktilL", [128, DC * 128], BF16)
                ktilB = asb("ktilB", [128, DC * 64], BF16)
                qt = asb("qt", [128, DC * 128], BF16)
                Am = asb("Am", [128, 2 * 128], BF16)
                w3 = asb("w3", [128, DC * 128], BF16)
                lx2 = asb("alx2", [128, 256])
                lrows = asb("alrows", [1, 512])
                ltscr = asb("altscr", [128, 128])

                psP = asx.enter_context(nc.psum_tensor("apsP", [128, 1024], F32))
                psS = asx.enter_context(nc.psum_tensor("apsS", [128, 512], F32))
                psAp = asx.enter_context(nc.psum_tensor("apsAp", [128, 1024], F32))
                psO = asx.enter_context(nc.psum_tensor("apsO", [128, 1024], F32))

                P.wait("gpsimd", t_prev)
                aw_sems = (s_wg[0], s_wu[0], s_wd[0], s_wg[1], s_wu[1])
                aw_toks = []
                for (nm, tile), s_aw in zip(
                        (("wkT", wk_sb), ("wvT", wv_sb), ("wrT", wr_sb),
                         ("wgT", wgg_sb), ("woT", wo_sb)), aw_sems):
                    aw_toks.append(P.op(
                        "gpsimd", "dma_start",
                        out=tile[:].rearrange("p (c e) -> p c e", e=D),
                        in_=ap[nm].rearrange("(c p) e -> p c e", p=128),
                        inc=(s_aw, 16)))

                P.op("gpsimd", "iota", Iq[:], pattern=[[1, 128]], base=1,
                     channel_multiplier=0, allow_small_or_imprecise_dtypes=True)
                P.op("gpsimd", "iota", IL[:], pattern=[[1, 64]], base=-32,
                     channel_multiplier=0, allow_small_or_imprecise_dtypes=True)
                P.op("gpsimd", "iota", IB96[:], pattern=[[-1, 64]], base=96,
                     channel_multiplier=0, allow_small_or_imprecise_dtypes=True)
                t_iq = P.op("gpsimd", "iota", revrow[:], pattern=[[-1, 128]],
                            base=127, channel_multiplier=0,
                            allow_small_or_imprecise_dtypes=True, inc=(s_gp, 1))
                P.waits("vector", t_iq, t_small)
                t_ptlast = None
                for ec in range(DC):
                    P.wait("vector", t_ptlast)
                    t_m = P.op("vector", "tensor_scalar", ptscr[:], Iq[:],
                               td_s[:, ec:ec + 1], None, ALU.mult,
                               inc=(s_dve, 1))
                    P.wait("scalar", t_m)
                    t_q = P.op("scalar", "activation", out=PTq[:, ts(ec, 128)],
                               in_=ptscr[:], func=AF.Exp, scale=-1.0,
                               inc=(s_act, 1))
                    P.wait("vector", t_q)
                    t_m = P.op("vector", "tensor_scalar", ptscr[:, 0:64],
                               IB96[:], td_s[:, ec:ec + 1], None, ALU.mult,
                               inc=(s_dve, 1))
                    P.wait("scalar", t_m)
                    t_b = P.op("scalar", "activation", out=PTtB[:, ts(ec, 64)],
                               in_=ptscr[:, 0:64], func=AF.Exp, scale=-1.0,
                               inc=(s_act, 1))
                    P.wait("vector", t_b)
                    P.op("vector", "tensor_scalar", ptscr[:, 0:64], IL[:],
                         td_s[:, ec:ec + 1], None, ALU.mult)
                    P.op("vector", "tensor_scalar", ptscr[:, 0:64],
                         ptscr[:, 0:64], -87.0, None, ALU.max)
                    t_m = P.op("vector", "tensor_scalar", ptscr[:, 0:64],
                               ptscr[:, 0:64], 87.0, None, ALU.min,
                               inc=(s_dve, 1))
                    P.wait("scalar", t_m)
                    for hf in range(2):
                        P.op("scalar", "activation",
                             out=PThL[:, ec * 128 + hf * 64:
                                      ec * 128 + hf * 64 + 64],
                             in_=ptscr[:, 0:64], func=AF.Exp, scale=-1.0)
                        t_ptlast = P.op("scalar", "activation",
                                        out=PTtL[:, ec * 128 + hf * 64:
                                                 ec * 128 + hf * 64 + 64],
                                        in_=ptscr[:, 0:64], func=AF.Exp,
                                        scale=1.0, inc=(s_act, 1))
                t_tfm = P.op("gpsimd", "memset", tf_full[:], 1.0, inc=(s_gp, 1))
                P.wait("vector", t_tfm)
                for ec in range(DC):
                    t_tff = P.op("vector", "tensor_scalar",
                                 tf_full[:, ts(ec, 128)],
                                 tf_full[:, ts(ec, 128)],
                                 tf_s[:, ec:ec + 1], None, ALU.mult,
                                 inc=(s_dve, 1))
                P.wait("tensor", t_iq)
                P.wait("tensor", t_prev)
                t_rp = P.op("tensor", "matmul", psP[:, 0:512], revrow[:],
                            td_row[0:1, 0:512], start=True, stop=True)
                t_rp2 = P.op("tensor", "matmul", psP[:, 512:1024], revrow[:],
                             td_row[0:1, 512:1024], start=True, stop=True,
                             inc=(s_pe, 1))
                P.wait("scalar", t_rp2)
                P.op("scalar", "activation", out=REVP[:, 0:512],
                     in_=psP[:, 0:512], func=AF.Exp, scale=-1.0)
                t_revp = P.op("scalar", "activation", out=REVP[:, 512:1024],
                              in_=psP[:, 512:1024], func=AF.Exp, scale=-1.0,
                              inc=(s_act, 1))
                t_e1 = P.op("scalar", "activation", out=e1[:], in_=td_s[:],
                            func=AF.Exp, scale=-128.0, inc=(s_act, 1))
                P.wait("vector", t_e1)
                P.op("vector", "memset", WCf[:], 1.0)
                for p in range(8):
                    t_wc = P.op("vector", "tensor_scalar", WCf[:, ts(p, 64)],
                                WCf[:, ts(p, 64)], e1[:, p:p + 1], None,
                                ALU.mult, inc=(s_dve, 1))

                ln_scope = {"psR": psP, "psAB": psP, "x2": lx2, "rows": lrows,
                            "tscr": ltscr, "rstride": 128, "x2w": 128}

                for apar_ in range(2):
                    P.op("vector", "memset",
                         psAp[64:128, apar_ * 512: apar_ * 512 + 64], 0.0)

                t_Strue = None
                t_psS_read = None
                prev_epi = None
                prev_sel = [None, None]
                t_wo_fl = None
                ps_ev = {0: None, 1: None}
                t_vtm = None

                for tc in range(cfg.n_tc):
                    if tc == 0:
                        P.waits("tensor", t_prev, t_wc, t_revp, t_ptlast,
                                *aw_toks)
                        ln_scope["t_pe_pre"] = t_prev
                    else:
                        ln_scope["t_pe_pre"] = None
                    P.waits("tensor", ps_ev[0], ps_ev[1])
                    t_xx = emit_ln_apply(ln_scope, (tc * 128, 128), xx, 128, 0,
                                         True)
                    P.wait("tensor", t_xx)
                    for wt, dst in ((wk_sb, Ktm), (wv_sb, Vtm)):
                        for half in range(2):
                            P.wait("tensor", ps_ev[half])
                            for dc in range(DC):
                                tmm = P.op("tensor", "matmul",
                                           psP[:, half * 512:(half + 1) * 512],
                                           xx[:, ts(dc, 128)],
                                           wt[:, dc * D + half * 512:
                                              dc * D + (half + 1) * 512],
                                           start=(dc == 0), stop=(dc == DC - 1),
                                           inc=(s_pe, 1) if dc == DC - 1 else None)
                            P.wait("scalar", tmm)
                            ps_ev[half] = P.op("scalar", "copy",
                                               dst[:, half * 512:(half + 1) * 512],
                                               psP[:, half * 512:(half + 1) * 512],
                                               inc=(s_act, 1))
                        if dst is Vtm:
                            t_vtm = ps_ev[1]
                    cnt = 0
                    for wt, dst, fn in ((wk_sb, Kfm, AF.Identity),
                                        (wv_sb, Vfm, AF.Identity),
                                        (wr_sb, Rfm, AF.Identity),
                                        (wgg_sb, Gfm, AF.Silu)):
                        for ec in range(DC):
                            half = cnt % 2
                            cnt += 1
                            P.wait("tensor", ps_ev[half])
                            for dc in range(DC):
                                tmm = P.op("tensor", "matmul",
                                           psP[:, half * 512: half * 512 + 128],
                                           wt[:, dc * D + ec * 128:
                                              dc * D + (ec + 1) * 128],
                                           xx[:, ts(dc, 128)],
                                           start=(dc == 0), stop=(dc == DC - 1),
                                           inc=(s_pe, 1) if dc == DC - 1 else None)
                            P.wait("scalar", tmm)
                            ps_ev[half] = P.op("scalar", "activation",
                                               out=dst[:, ts(ec, 128)],
                                               in_=psP[:, half * 512:
                                                       half * 512 + 128],
                                               func=fn, inc=(s_act, 1))
                    t_proj_done = ps_ev[(cnt - 1) % 2]
                    t_proj_done2 = ps_ev[cnt % 2]
                    P.waits("vector", t_proj_done, t_proj_done2)
                    t_k2 = P.op("vector", "tensor_tensor", k2[:], Ktm[:],
                                REVP[:], ALU.mult, inc=(s_dve, 1))
                    P.waits("tensor", t_k2, t_vtm, t_psS_read)
                    for h in range(H):
                        base = (h % 2) * 64
                        tmm = P.op("tensor", "matmul",
                                   psS[base:base + 64, ts(h // 2, 64)],
                                   k2[:, h * HD:(h + 1) * HD],
                                   Vtm[:, h * HD:(h + 1) * HD],
                                   start=True, stop=True,
                                   inc=(s_pe, 1) if h == H - 1 else None)
                    t_sadd = tmm
                    if tc == 0:
                        P.wait("vector", t_sadd)
                        P.op("vector", "tensor_copy", S_true[:], psS[:])
                        t_Strue = P.op("vector", "tensor_copy", S_bf[:],
                                       S_true[:], inc=(s_dve, 1))
                        t_psS_read = t_Strue
                        continue
                    kv3 = lambda t_: t_[:].rearrange("q (c t) -> q c t", t=128)
                    P.op("vector", "tensor_tensor", khatL[:], Kfm[:],
                         PThL[:], ALU.mult)
                    P.op("vector", "tensor_tensor", ktilL[:], Kfm[:],
                         PTtL[:], ALU.mult)
                    P.op("vector", "tensor_tensor",
                         ktilB[:].rearrange("q (c t) -> q c t", t=64),
                         kv3(Kfm)[:, :, 0:64],
                         PTtB[:].rearrange("q (c t) -> q c t", t=64), ALU.mult)
                    t_con = P.op("vector", "tensor_tensor", qt[:], Kfm[:],
                                 PTq[:], ALU.mult, inc=(s_dve, 1))
                    P.waits("tensor", t_con, prev_epi, t_Strue)
                    t_sel = dict()
                    for h in range(H):
                        base = (h % 2) * 64
                        ec = h // 2
                        apar = h % 2
                        c0 = apar * 512
                        P.wait("tensor", t_sel.get(h - 2, prev_sel[apar]))
                        P.op("tensor", "matmul",
                             psAp[0:64, c0:c0 + 64],
                             ktilL[base:base + 64, ec * 128: ec * 128 + 64],
                             khatL[base:base + 64, ec * 128: ec * 128 + 64],
                             start=True, stop=True)
                        P.op("tensor", "matmul",
                             psAp[0:64, c0 + 64:c0 + 128],
                             ktilB[base:base + 64, ts(ec, 64)],
                             khatL[base:base + 64,
                                   ec * 128 + 64: ec * 128 + 128],
                             start=True, stop=True)
                        t_amm = P.op("tensor", "matmul",
                                     psAp[64:128, c0 + 64:c0 + 128],
                                     ktilL[base:base + 64,
                                           ec * 128 + 64: ec * 128 + 128],
                                     khatL[base:base + 64,
                                           ec * 128 + 64: ec * 128 + 128],
                                     start=True, stop=True, inc=(s_pe, 1))
                        P.wait("vector", t_amm)
                        t_sel[h] = P.op("vector", "select", Am[:, ts(apar, 128)],
                                        tri_u8[:],
                                        psAp[:, apar * 512: apar * 512 + 128],
                                        zeros_bf[:], inc=(s_dve, 1))
                        P.wait("tensor", t_sel[h])
                        P.op("tensor", "matmul",
                             psO[base:base + 64, ts(ec, 128)],
                             Vtm[:, h * HD:(h + 1) * HD], Am[:, ts(apar, 128)],
                             start=True, stop=False)
                        t_out = P.op("tensor", "matmul",
                                     psO[base:base + 64, ts(ec, 128)],
                                     S_bf[base:base + 64, ts(ec, 64)],
                                     qt[base:base + 64, ts(ec, 128)],
                                     start=False, stop=True,
                                     inc=(s_pe, 1) if h == H - 1 else None)
                    prev_sel = [t_sel[H - 2], t_sel[H - 1]]
                    t_outs = t_out
                    P.wait("vector", t_outs)
                    P.op("vector", "tensor_tensor", S_true[:], S_true[:],
                         WCf[:], ALU.mult)
                    t_Strue = P.op("vector", "tensor_tensor", S_true[:],
                                   S_true[:], psS[:], ALU.add, inc=(s_dve, 1))
                    t_psS_read = t_Strue
                    t_Strue = P.op("vector", "tensor_copy", S_bf[:], S_true[:],
                                   inc=(s_dve, 1))
                    P.op("vector", "tensor_tensor", xx[:], Kfm[:], Vfm[:],
                         ALU.mult)
                    P.op("vector", "tensor_tensor", xx[:], xx[:], tf_full[:],
                         ALU.mult)
                    P.op("vector", "tensor_tensor", psO[:, 0:1024],
                         psO[:, 0:1024], xx[:], ALU.add)
                    P.op("vector", "tensor_tensor", psO[:, 0:1024],
                         psO[:, 0:1024], Rfm[:], ALU.mult)
                    t_w3 = P.op("vector", "tensor_tensor", w3[:],
                                psO[:, 0:1024], Gfm[:], ALU.mult,
                                inc=(s_dve, 1))
                    if "at1" in cfg.debug and tc == 1:
                        toks = []
                        P.waits("sync", t_w3, t_Strue)
                        for nm_, tile_, dt_ in (
                                ("at_xx", xx, BF16), ("at_Ktm", Ktm, BF16),
                                ("at_Vtm", Vtm, BF16), ("at_Kfm", Kfm, BF16),
                                ("at_Vfm", Vfm, BF16), ("at_Rfm", Rfm, BF16),
                                ("at_Gfm", Gfm, BF16), ("at_khat", khatL, BF16),
                                ("at_ktil", ktilL, BF16), ("at_qt", qt, BF16),
                                ("at_w3", w3, BF16), ("at_Sbf", S_bf, BF16),
                                ("at_k2", k2, BF16), ("at_Am", Am, BF16),
                                ("at_PTh", PThL, F32), ("at_REVP", REVP, F32),
                                ("at_WCf", WCf, F32)):
                            d_ = dbg(nm_, list(tile_[:].shape), dt_)
                            if d_ is not None:
                                tk = P.op("sync", "dma_start", out=d_[:, :],
                                          in_=tile_[:], inc=(s_st, 16))
                                toks.append(tk)
                        if toks:
                            P.wait("vector", toks[-1])
                            P.wait("tensor", toks[-1])
                    P.wait("tensor", t_w3)
                    for eo in range(DC):
                        for dc in range(DC):
                            tmm = P.op("tensor", "matmul",
                                       psO[:, eo * 128:(eo + 1) * 128],
                                       wo_sb[:, dc * D + eo * 128:
                                             dc * D + (eo + 1) * 128],
                                       w3[:, ts(dc, 128)],
                                       start=(dc == 0), stop=(dc == DC - 1),
                                       inc=(s_pe, 1)
                                       if (eo == DC - 1 and dc == DC - 1)
                                       else None)
                    P.wait("vector", tmm)
                    xv = x_sb[:].rearrange("q (c t) -> q c t", t=T)[
                        :, :, tc * 128:(tc + 1) * 128]
                    t_wo_fl = P.op("vector", "tensor_tensor", xv, xv,
                                   psO[:, 0:1024].rearrange(
                                       "q (c t) -> q c t", t=128),
                                   ALU.add, inc=(s_dve, 1))
                    prev_epi = t_wo_fl
                d = dbg("x_after_2", [128, DC * T])
                if d is not None:
                    P.wait("sync", t_wo_fl)
                    P.op("sync", "dma_start", out=d[:, :], in_=x_sb[:],
                         inc=(s_st, 16))
                return t_wo_fl

        tdone = None
        if "1" in cfg.phases:
            tdone = emit_ffn(1, tdone)
            P.flush(block)
        if "2" in cfg.phases:
            tdone = emit_attn(tdone)
            P.flush(block)
        if "3" in cfg.phases:
            tdone = emit_ffn(3, tdone)
        P.wait("sync", tdone)
        P.op("sync", "dma_start",
             out=ap["y"].rearrange("(c p) t -> p c t", p=128),
             in_=x_sb[:].rearrange("p (c t) -> p c t", t=T)[:, :, W:T],
             inc=(s_st, 16))
        P.wait("sync", (s_st, s_st.count))
        P.flush(block)

    return nc, list(dbg_outs)



def prep_host(cfg: Cfg, W_attn, ffn, td, tf, tm, gate_w, gate_bias):
    import ml_dtypes
    f = np.ascontiguousarray
    Wg_e, Wu_e, Wd_e, Sg, Su, Sd = [np.asarray(a, np.float32) for a in ffn]
    D = cfg.D
    gateT = f(np.concatenate([Wg_e.transpose(2, 0, 1).reshape(D, -1), Sg.T], axis=1))
    upT = f(np.concatenate([Wu_e.transpose(2, 0, 1).reshape(D, -1), Su.T], axis=1))
    downT = f(np.concatenate([Wd_e.transpose(0, 2, 1).reshape(-1, D), Sd.T], axis=0))
    assert gateT.shape == (D, cfg.NI) and downT.shape == (cfg.NI, D)
    eid = np.empty(cfg.NI, np.int64)
    eid[:cfg.NE * cfg.I_exp] = np.arange(cfg.NE * cfg.I_exp) // cfg.I_exp
    eid[cfg.NE * cfg.I_exp:] = cfg.NE
    m1T = np.zeros((cfg.NE + 1, cfg.NI), np.float32)
    m1T[eid, np.arange(cfg.NI)] = 1.0
    m1T = f(m1T.astype(ml_dtypes.bfloat16))

    td = np.asarray(td, np.float32).reshape(cfg.H, cfg.HD)
    tf = np.asarray(tf, np.float32).reshape(cfg.H, cfg.HD)
    tm = np.asarray(tm, np.float32).reshape(-1)
    td_s = f(td.reshape(cfg.H // 2, 2, cfg.HD).transpose(1, 2, 0).reshape(128, -1))
    tf_s = f(tf.reshape(cfg.H // 2, 2, cfg.HD).transpose(1, 2, 0).reshape(128, -1))
    tm_s = f(tm.reshape(cfg.DC, 128).T)
    Wk, Wv, Wr, Wgt, Wo = [np.asarray(a, np.float32) for a in W_attn]
    return {
        "gateT": gateT, "upT": upT, "downT": downT, "m1T": m1T,
        "wkT": f(Wk.T), "wvT": f(Wv.T), "wrT": f(Wr.T), "wgT": f(Wgt.T),
        "woT": f(Wo.T),
        "gwT": f(np.asarray(gate_w, np.float32).T),
        "td_s": td_s, "tf_s": tf_s, "tm_s": tm_s,
        "td_row": f(td.reshape(1, -1)),
        "gb_row": f(np.asarray(gate_bias, np.float32).reshape(1, -1)),
    }


def run_cores(cfg: Cfg, xT_full, host, **kw):
    nc, dbg_names = build_program(cfg)
    in_maps = []
    for c in range(cfg.n_cores):
        t0 = c * cfg.T_main
        halo = (np.zeros((cfg.D, cfg.W), np.float32) if c == 0
                else np.ascontiguousarray(xT_full[:, t0 - cfg.W:t0]))
        xc = np.concatenate([halo, xT_full[:, t0:t0 + cfg.T_main]], axis=1)
        m = {"xT": np.ascontiguousarray(xc)}
        m.update(host)
        in_maps.append(m)
    res = run_bass_kernel_spmd(nc, in_maps, core_ids=list(range(cfg.n_cores)),
                               **kw)
    return res


def kernel(**inputs):
    cfg = Cfg()
    x = np.asarray(inputs["L0_attn_x"], np.float32)
    Ttot = x.shape[1]
    assert Ttot == cfg.T_main * cfg.n_cores
    xT = np.ascontiguousarray(x[0].T)
    host = prep_host(
        cfg,
        (inputs["L0_attn_Wk"], inputs["L0_attn_Wv"], inputs["L0_attn_Wr"],
         inputs["L0_attn_Wg"], inputs["L0_attn_Wo"]),
        (inputs["L0_ffn_Wg"], inputs["L0_ffn_Wu"], inputs["L0_ffn_Wd"],
         inputs["L0_ffn_Sg"], inputs["L0_ffn_Su"], inputs["L0_ffn_Sd"]),
        inputs["L0_attn_time_decay"], inputs["L0_attn_time_first"],
        inputs["L0_attn_time_mix"], inputs["L0_ffn_gate_w"],
        inputs["L0_ffn_gate_bias"])
    res = run_cores(cfg, xT, host)
    y = np.concatenate([res.results[c]["y"] for c in range(cfg.n_cores)], axis=1)
    return np.ascontiguousarray(y.T).reshape(1, Ttot, cfg.D).astype(np.float32)


if __name__ == "__main__":
    cfg = Cfg()
    nc, _ = build_program(cfg)
    print("instructions:", sum(1 for _ in nc.all_instructions()))


# version 11
# speedup vs baseline: 1.0336x; 1.0336x over previous
import numpy as np
from contextlib import ExitStack

import concourse.bass as bass
import concourse.mybir as mybir
from concourse.bass_utils import run_bass_kernel_spmd

F32 = mybir.dt.float32
BF16 = mybir.dt.bfloat16
I8 = mybir.dt.int8
AF = mybir.ActivationFunctionType
ALU = mybir.AluOpType
AX = mybir.AxisListType

ENGINES = ("tensor", "vector", "scalar", "gpsimd", "sync")


class Cfg:
    def __init__(self, T_main=2048, W=128, NE=64, I_exp=704, SI=1408,
                 NT=512, GROUP=5, n_cores=8, debug=(), phases="123"):
        self.phases = phases
        self.T_main = T_main
        self.W = W
        self.T = T_main + W
        self.NE = NE
        self.I_exp = I_exp
        self.SI = SI
        self.NI = NE * I_exp + SI
        assert self.NI % 128 == 0
        self.IC = self.NI // 128
        self.NT = NT
        self.GROUP = GROUP
        self.D = 1024
        self.DC = 8
        self.H = 16
        self.HD = 64
        self.n_cores = n_cores
        self.debug = tuple(debug)
        assert T_main % NT == 0 and W % 128 == 0
        self.t_tiles_1 = ([(0, W)] if W else []) + \
            [(W + i * NT, NT) for i in range(T_main // NT)]
        self.t_tiles_3 = [(W + i * NT, NT) for i in range(T_main // NT)]
        self.n_tc = self.T // 128
        self.groups = []
        c = 0
        while c < self.IC:
            n = min(GROUP, self.IC - c)
            self.groups.append((c, n))
            c += n


class Sem:
    __slots__ = ("name", "h", "count")

    def __init__(self, name, h):
        self.name, self.h, self.count = name, h, 0


GP_COMPUTE = {"memset", "iota", "affine_select", "tensor_copy",
              "tensor_tensor", "tensor_scalar", "tensor_reduce"}


class Prog:

    def __init__(self, nc):
        self.nc = nc
        self.q = {e: [] for e in ENGINES}
        self.last_wait = {e: {} for e in ENGINES}
        self.auto = {}
        self.last_comp = {e: None for e in ENGINES}

    def set_auto(self, engine, sem):
        self.auto[engine] = sem

    def op(self, engine, method, *args, inc=None, **kwargs):
        chain = (engine in ("vector", "scalar")
                 or (engine == "gpsimd" and method in GP_COMPUTE))
        if chain and inc is None and engine in self.auto:
            inc = (self.auto[engine], 1)
        if chain:
            self.wait(engine, self.last_comp[engine])
        tok = None
        if inc is not None:
            sem, amt = inc
            sem.count += amt
            tok = (sem, sem.count)

        def fn(eng, method=method, args=args, kwargs=kwargs, inc=inc):
            inst = getattr(eng, method)(*args, **kwargs)
            if inc is not None:
                inst.then_inc(inc[0].h, inc[1])
        self.q[engine].append(fn)
        if chain:
            self.last_comp[engine] = tok
        return tok

    def wait(self, engine, tok):
        if tok is None:
            return
        sem, val = tok
        if self.last_wait[engine].get(sem.name, 0) >= val:
            return
        self.last_wait[engine][sem.name] = val

        def fn(eng, sem=sem, val=val):
            eng.wait_ge(sem.h, val)
        self.q[engine].append(fn)

    def waits(self, engine, *toks):
        for t in toks:
            self.wait(engine, t)

    def flush(self, block):
        for e in ENGINES:
            ops = self.q[e]
            if not ops:
                continue
            self.q[e] = []

            def runner(eng, ops=ops):
                for f in ops:
                    f(eng)
            getattr(block, e)(runner)


def ts(i, n):
    return slice(i * n, (i + 1) * n)



def build_program(cfg: Cfg):
    nc = bass.Bass("TRN2", target_bir_lowering=False)
    D, DC, NT, W, T = cfg.D, cfg.DC, cfg.NT, cfg.W, cfg.T
    NI = cfg.NI
    NEx = cfg.NE + 1
    H, HD = cfg.H, cfg.HD
    LNEPS = 1e-5

    ap = {}

    def din(name, shape, dt=F32):
        ap[name] = nc.dram_tensor(name, shape, dt, kind="ExternalInput").ap()

    din("xT", [D, T])
    din("gateT", [D, NI])
    din("upT", [D, NI])
    din("downT", [NI, D])
    din("m1T", [NEx, NI], BF16)
    for nm in ("wkT", "wvT", "wrT", "wgT", "woT"):
        din(nm, [D, D])
    din("gwT", [D, cfg.NE])
    din("td_s", [128, 8])
    din("tf_s", [128, 8])
    din("tm_s", [128, 8])
    din("td_row", [1, D])
    din("gb_row", [1, cfg.NE])
    ap["y"] = nc.dram_tensor("y", [D, cfg.T_main], F32, kind="ExternalOutput").ap()

    dbg_outs = {}

    def dbg(name, shape, dt=F32):
        if name in cfg.debug:
            if name not in dbg_outs:
                dbg_outs[name] = nc.dram_tensor("dbg_" + name, shape, dt,
                                                kind="ExternalOutput").ap()
            return dbg_outs[name]
        return None

    P = Prog(nc)

    with ExitStack() as g:
        sbuf = lambda n, s, dt=F32: g.enter_context(nc.sbuf_tensor(n, s, dt))
        sem = lambda n: Sem(n, g.enter_context(nc.semaphore(n)))

        x_sb = sbuf("x_sb", [128, DC * T])
        ident_bf = sbuf("ident_bf", [128, 128], BF16)
        tri_u8 = sbuf("tri_u8", [128, 128], I8)
        zeros_bf = sbuf("zeros_bf", [128, 128], BF16)
        ones_col = sbuf("ones_col", [128, 1])
        ones_row = sbuf("ones_row", [1, 128])
        td_s = sbuf("td_s_sb", [128, 8])
        tf_s = sbuf("tf_s_sb", [128, 8])
        tm_s = sbuf("tm_s_sb", [128, 8])
        td_row = sbuf("td_row_sb", [1, D])
        gb_sb = sbuf("gb_sb", [1, cfg.NE])
        gbrep = sbuf("gbrep", [128, cfg.NE])
        gwT_sb = sbuf("gwT_sb", [128, DC * cfg.NE], BF16)
        gscr = sbuf("gscr", [128, 128])

        s_ld = sem("s_ld")
        s_wg = [sem("s_wg0"), sem("s_wg1")]
        s_wu = [sem("s_wu0"), sem("s_wu1")]
        s_wd = [sem("s_wd0"), sem("s_wd1")]
        s_m = [sem("s_m0"), sem("s_m1")]
        s_wa = sem("s_wa")
        s_pe = sem("s_pe")
        s_act = sem("s_act")
        s_dve = sem("s_dve")
        s_gp = sem("s_gp")
        s_st = sem("s_st")
        P.set_auto("vector", s_dve)
        P.set_auto("scalar", s_act)
        P.set_auto("gpsimd", s_gp)

        block = g.enter_context(nc.Block())

        P.op("sync", "dma_start",
             out=x_sb[:].rearrange("p (c t) -> p c t", t=T),
             in_=ap["xT"].rearrange("(c p) t -> p c t", p=128),
             inc=(s_ld, 16))
        for name, tile in (("td_s", td_s), ("tf_s", tf_s), ("tm_s", tm_s),
                           ("td_row", td_row), ("gb_row", gb_sb)):
            P.op("sync", "dma_start", out=tile[:], in_=ap[name][:, :],
                 inc=(s_ld, 16))
        t_small = (s_ld, s_ld.count)
        t_x = t_small
        t_gwT = P.op("gpsimd", "dma_start",
                     out=gwT_sb[:].rearrange("p (c e) -> p c e", e=cfg.NE),
                     in_=ap["gwT"].rearrange("(c p) e -> p c e", p=128),
                     inc=(s_wa, 16))

        P.op("gpsimd", "memset", ident_bf[:], 0.0)
        P.op("gpsimd", "affine_select", out=ident_bf[:], in_=ident_bf[:],
             compare_op=ALU.not_equal, fill=1.0, base=0,
             pattern=[[-1, 128]], channel_multiplier=1)
        P.op("gpsimd", "memset", zeros_bf[:], 0.0)
        P.op("gpsimd", "memset", gscr[:], 0.0)
        t_trif = P.op("gpsimd", "affine_select", out=gscr[:], in_=gscr[:],
                      compare_op=ALU.is_gt, fill=1.0, base=0,
                      pattern=[[-1, 128]], channel_multiplier=1)
        P.op("gpsimd", "memset", ones_col[:], 1.0)
        t_gp_const = P.op("gpsimd", "memset", ones_row[:], 1.0, inc=(s_gp, 1))
        P.wait("vector", t_gp_const)
        t_tri = P.op("vector", "tensor_copy", tri_u8[:], gscr[:])

        with nc.psum_tensor("ps_init", [128, 512], F32) as ps_init:
            P.waits("tensor", t_small, t_gp_const)
            t_mm = P.op("tensor", "matmul", ps_init[:, 0:cfg.NE], ones_row[:],
                        gb_sb[:], start=True, stop=True, inc=(s_pe, 1))
            P.wait("scalar", t_mm)
            t_gbrep = P.op("scalar", "copy", gbrep[:], ps_init[:, 0:cfg.NE],
                           inc=(s_act, 1))
            P.wait("tensor", t_gbrep)

        def emit_ln_apply(scope, x_cols, out_tile, out_w, out_off, with_tm):
            psR, psAB = scope["psR"], scope["psAB"]
            x2, rows, tscr = scope["x2"], scope["rows"], scope["tscr"]
            rs = scope.get("rstride", 512)
            xw = scope.get("x2w", 512)
            t0, tlen = x_cols
            last = None
            nblk = (tlen + NT - 1) // NT
            tm2_toks = {}
            for bi in range(nblk):
                b0 = t0 + bi * NT
                blen = min(NT, tlen - bi * NT)
                ob0 = out_off + bi * NT
                for dc in range(DC):
                    xs = x_sb[:, dc * T + b0: dc * T + b0 + blen]
                    par = dc % 2
                    P.wait("vector", tm2_toks.get((bi, dc - 2)))
                    tw = P.op("vector", "tensor_tensor",
                              x2[:, par * xw:par * xw + blen], xs, xs,
                              ALU.mult, inc=(s_dve, 1))
                    if scope.get("t_pe_pre") is not None and bi == 0 and dc == 0:
                        P.wait("tensor", scope["t_pe_pre"])
                    P.op("tensor", "matmul", psR[0:1, 0:blen], ones_col[:], xs,
                         start=(dc == 0), stop=(dc == DC - 1))
                    P.wait("tensor", tw)
                    tm2_toks[(bi, dc)] = P.op(
                        "tensor", "matmul", psR[0:1, 512:512 + blen], ones_col[:],
                        x2[:, par * xw:par * xw + blen],
                        start=(dc == 0), stop=(dc == DC - 1), inc=(s_pe, 1))
                t_stats = tm2_toks[(bi, DC - 1)]
                P.wait("vector", t_stats)
                mean = rows[0:1, 0:blen]
                msq = rows[0:1, rs:rs + blen]
                Arow = rows[0:1, 2 * rs:2 * rs + blen]
                Brow = rows[0:1, 3 * rs:3 * rs + blen]
                P.op("vector", "tensor_scalar", mean, psR[0:1, 0:blen],
                     1.0 / D, None, ALU.mult)
                P.op("vector", "tensor_scalar", msq, psR[0:1, 512:512 + blen],
                     1.0 / D, None, ALU.mult)
                P.op("vector", "tensor_tensor", Arow, mean, mean, ALU.mult)
                P.op("vector", "tensor_tensor", Arow, msq, Arow, ALU.subtract)
                t_v = P.op("vector", "tensor_scalar", Arow, Arow, LNEPS, None,
                           ALU.add, inc=(s_dve, 1))
                P.wait("scalar", t_v)
                P.op("scalar", "activation", out=Arow, in_=Arow, func=AF.Ln)
                t_a = P.op("scalar", "activation", out=Arow, in_=Arow,
                           func=AF.Exp, scale=-0.5, inc=(s_act, 1))
                P.wait("vector", t_a)
                t_b = P.op("vector", "tensor_tensor", Brow, mean, Arow,
                           ALU.mult, inc=(s_dve, 1))
                P.wait("tensor", t_b)
                ta = P.op("tensor", "matmul", psAB[:, 0:blen], ones_row[:],
                          Arow, start=True, stop=True)
                tb = P.op("tensor", "matmul", psAB[:, 512:512 + blen],
                          ones_row[:], Brow, start=True, stop=True,
                          inc=(s_pe, 1))
                P.wait("vector", tb)
                for dc in range(DC):
                    xs = x_sb[:, dc * T + b0: dc * T + b0 + blen]
                    P.op("vector", "tensor_tensor", tscr[:, 0:blen], xs,
                         psAB[:, 0:blen], ALU.mult)
                    if with_tm:
                        P.op("vector", "tensor_tensor", tscr[:, 0:blen],
                             tscr[:, 0:blen], psAB[:, 512:512 + blen],
                             ALU.subtract)
                        last = P.op("vector", "tensor_scalar",
                                    out_tile[:, dc * out_w + ob0:
                                             dc * out_w + ob0 + blen],
                                    tscr[:, 0:blen], tm_s[:, dc:dc + 1], None,
                                    ALU.mult, inc=(s_dve, 1))
                    else:
                        last = P.op("vector", "tensor_tensor",
                                    out_tile[:, dc * out_w + ob0:
                                             dc * out_w + ob0 + blen],
                                    tscr[:, 0:blen], psAB[:, 512:512 + blen],
                                    ALU.subtract, inc=(s_dve, 1))
                P.wait("tensor", last)
            return last

        def emit_ffn(call, t_prev):
            t_tiles = cfg.t_tiles_1 if call == 1 else cfg.t_tiles_3
            TW = T
            with ExitStack() as fs:
                fsb = lambda n, s, dt=F32: fs.enter_context(
                    nc.sbuf_tensor(f"{n}_c{call}", s, dt))
                xln = fsb("xln", [128, DC * TW], BF16)
                rw_fm = fsb("rw_fm", [128, TW], BF16)
                GW = cfg.GROUP * 128
                wg_b = fsb("wg_b", [128, 2 * DC * GW], BF16)
                wu_b = fsb("wu_b", [128, 2 * DC * GW], BF16)
                wd_b = fsb("wd_b", [128, 2 * cfg.GROUP * D], BF16)
                m1_b = fsb("m1_b", [NEx, 2 * GW], BF16)
                h_sb = fsb("h_sb", [128, 2 * cfg.GROUP * NT], BF16)
                sS = fsb("sS", [128, 2 * NT], BF16)
                t1 = fsb("t1", [128, 2 * NT])


                with ExitStack() as lns:
                    lpsR = lns.enter_context(
                        nc.psum_tensor(f"lnpsR_{call}", [1, 1024], F32))
                    lpsAB = lns.enter_context(
                        nc.psum_tensor(f"lnpsAB_{call}", [128, 1024], F32))
                    lx2 = lns.enter_context(
                        nc.sbuf_tensor(f"lx2_{call}", [128, 1024], F32))
                    lrows = lns.enter_context(
                        nc.sbuf_tensor(f"lrows_{call}", [1, 2048], F32))
                    ltscr = lns.enter_context(
                        nc.sbuf_tensor(f"ltscr_{call}", [128, 512], F32))
                    scope = {"psR": lpsR, "psAB": lpsAB, "x2": lx2,
                             "rows": lrows, "tscr": ltscr, "t_pe_pre": t_prev}
                    if call == 1:
                        P.wait("vector", t_x)
                        t_ln = emit_ln_apply(scope, (0, T), xln, TW, 0, False)
                    else:
                        t_ln = emit_ln_apply(scope, (W, cfg.T_main), xln, TW, W,
                                             False)

                    psRt = lns.enter_context(
                        nc.psum_tensor(f"rpsR_{call}", [128, 1024], F32))
                    psTp = lns.enter_context(
                        nc.psum_tensor(f"rpsT_{call}", [128, 128], BF16))
                    sc = lns.enter_context(
                        nc.sbuf_tensor(f"rsc_{call}", [128, 2 * 64], F32))
                    s4c = lns.enter_context(
                        nc.sbuf_tensor(f"rs4c_{call}", [128, 64], F32))
                    gsv = lns.enter_context(
                        nc.sbuf_tensor(f"rgs_{call}", [128, 8], F32))
                    m8 = lns.enter_context(
                        nc.sbuf_tensor(f"rm8_{call}", [128, 8], F32))
                    msk = lns.enter_context(
                        nc.sbuf_tensor(f"rmsk_{call}", [128, 64], F32))
                    gm = lns.enter_context(
                        nc.sbuf_tensor(f"rgm_{call}", [128, 8], F32))
                    rww = lns.enter_context(
                        nc.sbuf_tensor(f"rww_{call}", [128, 64], F32))
                    den = lns.enter_context(
                        nc.sbuf_tensor(f"rden_{call}", [128, 1], F32))
                    rwtm = lns.enter_context(
                        nc.sbuf_tensor(f"rwtm_{call}", [128, 64], BF16))

                    tc_range = list(range(cfg.n_tc)) if call == 1 else \
                        list(range(1, cfg.n_tc))
                    P.op("vector", "memset", rw_fm[64:65, 0:TW], 0.5)
                    P.wait("tensor", t_ln)
                    P.waits("tensor", t_gwT)
                    sc_tok = {0: None, 1: None}
                    prev_evac = None
                    prev_dve = None
                    for tc in tc_range:
                        par = tc % 2
                        P.wait("tensor", sc_tok[par])
                        for dc in range(DC):
                            tmm = P.op("tensor", "matmul",
                                       psRt[:, par * 512:par * 512 + cfg.NE],
                                       xln[:, dc * TW + tc * 128:
                                           dc * TW + (tc + 1) * 128],
                                       gwT_sb[:, ts(dc, cfg.NE)],
                                       start=(dc == 0), stop=(dc == DC - 1),
                                       inc=(s_pe, 1) if dc == DC - 1 else None)
                        P.wait("scalar", tmm)
                        P.wait("scalar", prev_dve)
                        sc_tok[par] = P.op("scalar", "activation",
                                           out=sc[:, ts(par, 64)],
                                           in_=psRt[:, par * 512:par * 512 + cfg.NE],
                                           func=AF.Sigmoid, inc=(s_act, 1))
                        P.wait("vector", sc_tok[par])
                        scv = sc[:, ts(par, 64)]
                        P.op("vector", "tensor_tensor", s4c[:], scv, gbrep[:],
                             ALU.add)
                        for gi in range(8):
                            P.op("vector", "max", out=m8[:], in_=s4c[:, ts(gi, 8)])
                            P.op("vector", "tensor_tensor", gsv[:, gi:gi + 1],
                                 m8[:, 0:1], m8[:, 1:2], ALU.add)
                        P.op("vector", "max", out=m8[:], in_=gsv[:])
                        P.op("vector", "tensor_scalar", gm[:], gsv[:],
                             m8[:, 3:4], None, ALU.is_ge)
                        for gi in range(8):
                            P.op("vector", "tensor_scalar", msk[:, ts(gi, 8)],
                                 s4c[:, ts(gi, 8)], gm[:, gi:gi + 1], None,
                                 ALU.mult)
                        P.op("vector", "max", out=m8[:], in_=msk[:])
                        P.op("vector", "tensor_scalar", msk[:], msk[:],
                             m8[:, 7:8], None, ALU.is_ge)
                        P.op("vector", "tensor_tensor", rww[:], scv, msk[:],
                             ALU.mult)
                        P.op("vector", "tensor_reduce", den[:], rww[:],
                             axis=AX.X, op=ALU.add)
                        P.op("vector", "tensor_scalar", den[:], den[:], 1e-20,
                             None, ALU.add)
                        P.op("vector", "reciprocal", den[:], den[:])
                        P.op("vector", "tensor_scalar", den[:], den[:], 1.25,
                             None, ALU.mult)
                        t_rwtm = P.op("vector", "tensor_scalar", rwtm[:],
                                      rww[:], den[:], None, ALU.mult,
                                      inc=(s_dve, 1))
                        prev_dve = t_rwtm
                        P.wait("tensor", t_rwtm)
                        P.wait("tensor", prev_evac)
                        t_tp = P.op("tensor", "transpose", psTp[0:64, :],
                                    rwtm[:], ident_bf[:], inc=(s_pe, 1))
                        P.wait("vector", t_tp)
                        prev_evac = P.op("vector", "tensor_copy",
                                         rw_fm[0:64, tc * 128:(tc + 1) * 128],
                                         psTp[0:64, :], inc=(s_dve, 1))
                    t_rw_ready = prev_evac
                    if call == 1:
                        for nm_, tile_, shape_, dt_ in (
                                ("rt_sc", sc, [128, 128], F32),
                                ("rt_s4c", s4c, [128, 64], F32),
                                ("rt_msk", msk, [128, 64], F32),
                                ("rt_rww", rww, [128, 64], F32),
                                ("rt_den", den, [128, 1], F32),
                                ("rt_m8", m8, [128, 8], F32),
                                ("rt_gm", gm, [128, 8], F32),
                                ("rt_gsv", gsv, [128, 8], F32),
                                ("rt_rwtm", rwtm, [128, 64], BF16),
                                ("rt_gbrep", gbrep, [128, 64], F32)):
                            d_ = dbg(nm_, shape_, dt_)
                            if d_ is not None:
                                P.wait("sync", t_rw_ready)
                                P.op("sync", "dma_start", out=d_[:, :],
                                     in_=tile_[:], inc=(s_st, 16))
                    d = dbg(f"rw_{call}", [128, TW], BF16)
                    if d is not None:
                        P.wait("sync", t_rw_ready)
                        P.op("sync", "dma_start", out=d[:, :], in_=rw_fm[:],
                             inc=(s_st, 16))
                    d = dbg(f"xln_{call}", [128, DC * TW], BF16)
                    if d is not None:
                        P.wait("sync", t_ln)
                        P.op("sync", "dma_start", out=d[:, :], in_=xln[:],
                             inc=(s_st, 16))

                psA = fs.enter_context(nc.psum_tensor(f"psA_{call}", [128, 1024], F32))
                psB = fs.enter_context(nc.psum_tensor(f"psB_{call}", [128, 1024], F32))
                psC = fs.enter_context(nc.psum_tensor(f"psC_{call}", [128, 1024], F32))
                psD = fs.enter_context(nc.psum_tensor(f"psD_{call}", [128, 1024], F32))

                gateT_v = ap["gateT"].rearrange("(c p) i -> p c i", p=128)
                upT_v = ap["upT"].rearrange("(c p) i -> p c i", p=128)
                downT_v = ap["downT"].rearrange("(g p) d -> p g d", p=128)

                w_tok = {}
                free_tok = {0: t_prev, 1: t_prev}

                def emit_group_piece(gi, piece, gate_tok):
                    c0, ncg = cfg.groups[gi]
                    par = gi % 2
                    i0, wlen = c0 * 128, ncg * 128
                    P.wait("gpsimd", gate_tok)
                    if piece == 0:
                        P.wait("gpsimd", free_tok[par])
                        P.op("gpsimd", "dma_start",
                             out=wg_b[:, par * DC * GW: (par + 1) * DC * GW]
                             .rearrange("p (c i) -> p c i", i=GW)[:, :, 0:wlen],
                             in_=gateT_v[:, :, i0:i0 + wlen],
                             inc=(s_wg[par], 16))
                        P.wait("sync", free_tok[par])
                        P.wait("sync", gate_tok)
                        P.op("sync", "dma_start",
                             out=m1_b[:, par * GW: par * GW + wlen],
                             in_=ap["m1T"][:, i0:i0 + wlen],
                             inc=(s_m[par], 16))
                    elif piece == 1:
                        P.op("gpsimd", "dma_start",
                             out=wu_b[:, par * DC * GW: (par + 1) * DC * GW]
                             .rearrange("p (c i) -> p c i", i=GW)[:, :, 0:wlen],
                             in_=upT_v[:, :, i0:i0 + wlen],
                             inc=(s_wu[par], 16))
                    else:
                        P.op("gpsimd", "dma_start",
                             out=wd_b[:, par * cfg.GROUP * D:
                                      par * cfg.GROUP * D + ncg * D]
                             .rearrange("p (c d) -> p c d", d=D),
                             in_=downT_v[:, c0:c0 + ncg, :],
                             inc=(s_wd[par], 16))
                    w_tok[gi] = tuple((s_, s_.count) for s_ in
                                      (s_wg[par], s_wu[par], s_wd[par],
                                       s_m[par]))

                def issue_group_dma(gi):
                    for pc in range(3):
                        emit_group_piece(gi, pc, None)

                issue_group_dma(0)
                if len(cfg.groups) > 1:
                    issue_group_dma(1)
                pending = []

                silu_tok = [None, None]
                h_tok = [None, None]
                flush_tok = [None, None]
                psC_last = [None, None]
                sS_last = [None, None]
                h_guard = [None, None]
                t_first_wait_done = False
                t_fl = None

                for gi, (c0, ncg) in enumerate(cfg.groups):
                    par = gi % 2
                    gbase_g = par * DC * GW
                    gbase_d = par * cfg.GROUP * D
                    gbase_m = par * GW
                    gbase_h = par * cfg.GROUP * NT
                    P.waits("tensor", *w_tok[gi])
                    for (tt0, tlen) in t_tiles:
                        for ci in range(ncg):
                            cpar = ci % 2
                            if not t_first_wait_done:
                                P.wait("tensor", t_rw_ready)
                                t_first_wait_done = True
                            P.wait("tensor", psC_last[cpar])
                            t_rwmm = P.op("tensor", "matmul",
                                          psC[:, cpar * 512: cpar * 512 + tlen],
                                          m1_b[:, gbase_m + ci * 128:
                                               gbase_m + (ci + 1) * 128],
                                          rw_fm[0:NEx, tt0:tt0 + tlen],
                                          start=True, stop=True, inc=(s_pe, 1))
                            P.wait("tensor", silu_tok[cpar])
                            for dc in range(DC):
                                t_gmm = P.op("tensor", "matmul",
                                             psA[:, cpar * 512: cpar * 512 + tlen],
                                             wg_b[:, gbase_g + dc * GW + ci * 128:
                                                  gbase_g + dc * GW + (ci + 1) * 128],
                                             xln[:, dc * TW + tt0:
                                                 dc * TW + tt0 + tlen],
                                             start=(dc == 0), stop=(dc == DC - 1),
                                             inc=(s_pe, 1) if dc == DC - 1 else None)
                            P.wait("tensor", h_tok[cpar])
                            for dc in range(DC):
                                t_umm = P.op("tensor", "matmul",
                                             psB[:, cpar * 512: cpar * 512 + tlen],
                                             wu_b[:, gbase_g + dc * GW + ci * 128:
                                                  gbase_g + dc * GW + (ci + 1) * 128],
                                             xln[:, dc * TW + tt0:
                                                 dc * TW + tt0 + tlen],
                                             start=(dc == 0), stop=(dc == DC - 1),
                                             inc=(s_pe, 1) if dc == DC - 1 else None)
                            P.wait("scalar", t_gmm)
                            P.wait("scalar", sS_last[cpar])
                            t_silu = P.op("scalar", "activation",
                                          out=sS[:, cpar * NT: cpar * NT + tlen],
                                          in_=psA[:, cpar * 512: cpar * 512 + tlen],
                                          func=AF.Silu, inc=(s_act, 1))
                            silu_tok[cpar] = t_silu
                            P.waits("vector", t_silu, t_rwmm)
                            t_t1 = P.op("vector", "tensor_tensor",
                                        t1[:, cpar * NT: cpar * NT + tlen],
                                        sS[:, cpar * NT: cpar * NT + tlen],
                                        psC[:, cpar * 512: cpar * 512 + tlen],
                                        ALU.mult, inc=(s_dve, 1))
                            psC_last[cpar] = t_t1
                            sS_last[cpar] = t_t1
                            P.wait("vector", t_umm)
                            P.wait("vector", h_guard[par])
                            t_hw = P.op("vector", "tensor_tensor",
                                        h_sb[:, gbase_h + ci * NT:
                                             gbase_h + ci * NT + tlen],
                                        t1[:, cpar * NT: cpar * NT + tlen],
                                        psB[:, cpar * 512: cpar * 512 + tlen],
                                        ALU.mult, inc=(s_dve, 1))
                            h_tok[cpar] = t_hw
                            last_h = t_hw
                        P.wait("tensor", last_h)
                        for mm in range(8):
                            dpar = mm % 2
                            P.wait("tensor", flush_tok[dpar])
                            for ci in range(ncg):
                                tmm = P.op("tensor", "matmul",
                                           psD[:, dpar * 512: dpar * 512 + tlen],
                                           wd_b[:, gbase_d + ci * D + mm * 128:
                                                gbase_d + ci * D + (mm + 1) * 128],
                                           h_sb[:, gbase_h + ci * NT:
                                                gbase_h + ci * NT + tlen],
                                           start=(ci == 0), stop=(ci == ncg - 1),
                                           inc=(s_pe, 1)
                                           if ci == ncg - 1 else None)
                            P.wait("vector", tmm)
                            xs = x_sb[:, mm * T + tt0: mm * T + tt0 + tlen]
                            t_fl = P.op("vector", "tensor_tensor", xs, xs,
                                        psD[:, dpar * 512: dpar * 512 + tlen],
                                        ALU.add, inc=(s_dve, 1))
                            flush_tok[dpar] = t_fl
                        h_guard[par] = tmm
                        if pending:
                            gi2, pc = pending.pop(0)
                            emit_group_piece(gi2, pc, tmm)
                    while pending:
                        gi2, pc = pending.pop(0)
                        emit_group_piece(gi2, pc, h_guard[par])
                    if gi + 2 < len(cfg.groups):
                        free_tok[par] = h_guard[par]
                        pending = [(gi + 2, 0), (gi + 2, 1), (gi + 2, 2)]
                t_done = t_fl
                if call == 1:
                    for nm_, tile_, dt_ in (("ff_wg", wg_b, BF16),
                                            ("ff_wu", wu_b, BF16),
                                            ("ff_wd", wd_b, BF16),
                                            ("ff_m1", m1_b, BF16),
                                            ("ff_h", h_sb, BF16),
                                            ("ff_sS", sS, BF16),
                                            ("ff_t1", t1, F32)):
                        d_ = dbg(nm_, list(tile_[:].shape), dt_)
                        if d_ is not None:
                            P.wait("sync", t_done)
                            P.op("sync", "dma_start", out=d_[:, :],
                                 in_=tile_[:], inc=(s_st, 16))
                d = dbg(f"x_after_{call}", [128, DC * T])
                if d is not None:
                    P.wait("sync", t_done)
                    P.op("sync", "dma_start", out=d[:, :], in_=x_sb[:],
                         inc=(s_st, 16))
                return t_done

        def emit_attn(t_prev):
            with ExitStack() as asx:
                asb = lambda n, s, dt=F32: asx.enter_context(
                    nc.sbuf_tensor(n, s, dt))
                wk_sb = asb("wk_sb", [128, DC * D], BF16)
                wv_sb = asb("wv_sb", [128, DC * D], BF16)
                wr_sb = asb("wr_sb", [128, DC * D], BF16)
                wgg_sb = asb("wgg_sb", [128, DC * D], BF16)
                wo_sb = asb("wo_sb", [128, DC * D], BF16)
                PThL = asb("PThL", [128, DC * 128], BF16)
                PTtL = asb("PTtL", [128, DC * 128], BF16)
                PTtB = asb("PTtB", [128, DC * 64], BF16)
                PTq = asb("PTq", [128, DC * 128], BF16)
                tf_full = asb("tf_full", [128, DC * 128], BF16)
                REVP = asb("REVP", [128, D])
                WCf = asb("WCf", [128, 512])
                IL = asb("IL", [128, 64])
                IB96 = asb("IB96", [128, 64])
                Iq = asb("Iq", [128, 128])
                revrow = asb("revrow", [1, 128])
                e1 = asb("e1", [128, 8])
                ptscr = asb("ptscr", [128, 128])
                S_true = asb("S_true", [128, 512])
                S_bf = asb("S_bf", [128, 512], BF16)
                xx = asb("xx", [128, DC * 128], BF16)
                Kfm = asb("Kfm", [128, DC * 128], BF16)
                Vfm = asb("Vfm", [128, DC * 128], BF16)
                Rfm = asb("Rfm", [128, DC * 128], BF16)
                Gfm = asb("Gfm", [128, DC * 128], BF16)
                Ktm = asb("Ktm", [128, D], BF16)
                Vtm = asb("Vtm", [128, D], BF16)
                k2 = asb("k2", [128, D], BF16)
                khatL = asb("khatL", [128, DC * 128], BF16)
                ktilL = asb("ktilL", [128, DC * 128], BF16)
                ktilB = asb("ktilB", [128, DC * 64], BF16)
                qt = asb("qt", [128, DC * 128], BF16)
                Am = asb("Am", [128, 2 * 128], BF16)
                w3 = asb("w3", [128, DC * 128], BF16)
                lx2 = asb("alx2", [128, 256])
                lrows = asb("alrows", [1, 512])
                ltscr = asb("altscr", [128, 128])

                psP = asx.enter_context(nc.psum_tensor("apsP", [128, 1024], F32))
                psS = asx.enter_context(nc.psum_tensor("apsS", [128, 512], F32))
                psAp = asx.enter_context(nc.psum_tensor("apsAp", [128, 1024], F32))
                psO = asx.enter_context(nc.psum_tensor("apsO", [128, 1024], F32))

                P.wait("gpsimd", t_prev)
                aw_sems = (s_wg[0], s_wu[0], s_wd[0], s_wg[1], s_wu[1])
                aw_toks = []
                for (nm, tile), s_aw in zip(
                        (("wkT", wk_sb), ("wvT", wv_sb), ("wrT", wr_sb),
                         ("wgT", wgg_sb), ("woT", wo_sb)), aw_sems):
                    aw_toks.append(P.op(
                        "gpsimd", "dma_start",
                        out=tile[:].rearrange("p (c e) -> p c e", e=D),
                        in_=ap[nm].rearrange("(c p) e -> p c e", p=128),
                        inc=(s_aw, 16)))

                P.op("gpsimd", "iota", Iq[:], pattern=[[1, 128]], base=1,
                     channel_multiplier=0, allow_small_or_imprecise_dtypes=True)
                P.op("gpsimd", "iota", IL[:], pattern=[[1, 64]], base=-32,
                     channel_multiplier=0, allow_small_or_imprecise_dtypes=True)
                P.op("gpsimd", "iota", IB96[:], pattern=[[-1, 64]], base=96,
                     channel_multiplier=0, allow_small_or_imprecise_dtypes=True)
                t_iq = P.op("gpsimd", "iota", revrow[:], pattern=[[-1, 128]],
                            base=127, channel_multiplier=0,
                            allow_small_or_imprecise_dtypes=True, inc=(s_gp, 1))
                P.waits("vector", t_iq, t_small)
                t_ptlast = None
                for ec in range(DC):
                    P.wait("vector", t_ptlast)
                    LN2x30 = 20.794415416798359
                    t_m = P.op("vector", "tensor_scalar", ptscr[:], Iq[:],
                               td_s[:, ec:ec + 1], None, ALU.mult)
                    t_m = P.op("vector", "tensor_scalar", ptscr[:], ptscr[:],
                               2 * LN2x30, None, ALU.add, inc=(s_dve, 1))
                    P.wait("scalar", t_m)
                    t_q = P.op("scalar", "activation", out=PTq[:, ts(ec, 128)],
                               in_=ptscr[:], func=AF.Exp, scale=-1.0,
                               inc=(s_act, 1))
                    P.wait("vector", t_q)
                    t_m = P.op("vector", "tensor_scalar", ptscr[:, 0:64],
                               IB96[:], td_s[:, ec:ec + 1], None, ALU.mult)
                    t_m = P.op("vector", "tensor_scalar", ptscr[:, 0:64],
                               ptscr[:, 0:64], LN2x30, None, ALU.add,
                               inc=(s_dve, 1))
                    P.wait("scalar", t_m)
                    t_b = P.op("scalar", "activation", out=PTtB[:, ts(ec, 64)],
                               in_=ptscr[:, 0:64], func=AF.Exp, scale=-1.0,
                               inc=(s_act, 1))
                    P.wait("vector", t_b)
                    P.op("vector", "tensor_scalar", ptscr[:, 0:64], IL[:],
                         td_s[:, ec:ec + 1], None, ALU.mult)
                    P.op("vector", "tensor_scalar", ptscr[:, 0:64],
                         ptscr[:, 0:64], -87.0, None, ALU.max)
                    P.op("vector", "tensor_scalar", ptscr[:, 0:64],
                         ptscr[:, 0:64], 87.0, None, ALU.min)
                    P.op("vector", "tensor_scalar", ptscr[:, 64:128],
                         ptscr[:, 0:64], LN2x30, None, ALU.add)
                    t_m = P.op("vector", "tensor_scalar", ptscr[:, 0:64],
                               ptscr[:, 0:64], -LN2x30, None, ALU.add,
                               inc=(s_dve, 1))
                    P.wait("scalar", t_m)
                    for hf in range(2):
                        P.op("scalar", "activation",
                             out=PThL[:, ec * 128 + hf * 64:
                                      ec * 128 + hf * 64 + 64],
                             in_=ptscr[:, 64:128], func=AF.Exp, scale=-1.0)
                        t_ptlast = P.op("scalar", "activation",
                                        out=PTtL[:, ec * 128 + hf * 64:
                                                 ec * 128 + hf * 64 + 64],
                                        in_=ptscr[:, 0:64], func=AF.Exp,
                                        scale=1.0, inc=(s_act, 1))
                t_tfm = P.op("gpsimd", "memset", tf_full[:], 1.0, inc=(s_gp, 1))
                P.wait("vector", t_tfm)
                for ec in range(DC):
                    t_tff = P.op("vector", "tensor_scalar",
                                 tf_full[:, ts(ec, 128)],
                                 tf_full[:, ts(ec, 128)],
                                 tf_s[:, ec:ec + 1], None, ALU.mult,
                                 inc=(s_dve, 1))
                P.wait("tensor", t_iq)
                P.wait("tensor", t_prev)
                t_rp = P.op("tensor", "matmul", psP[:, 0:512], revrow[:],
                            td_row[0:1, 0:512], start=True, stop=True)
                t_rp2 = P.op("tensor", "matmul", psP[:, 512:1024], revrow[:],
                             td_row[0:1, 512:1024], start=True, stop=True,
                             inc=(s_pe, 1))
                P.wait("scalar", t_rp2)
                P.op("scalar", "activation", out=REVP[:, 0:512],
                     in_=psP[:, 0:512], func=AF.Exp, scale=-1.0)
                t_revp = P.op("scalar", "activation", out=REVP[:, 512:1024],
                              in_=psP[:, 512:1024], func=AF.Exp, scale=-1.0,
                              inc=(s_act, 1))
                t_e1 = P.op("scalar", "activation", out=e1[:], in_=td_s[:],
                            func=AF.Exp, scale=-128.0, inc=(s_act, 1))
                P.wait("vector", t_e1)
                P.op("vector", "memset", WCf[:], 1.0)
                for p in range(8):
                    t_wc = P.op("vector", "tensor_scalar", WCf[:, ts(p, 64)],
                                WCf[:, ts(p, 64)], e1[:, p:p + 1], None,
                                ALU.mult, inc=(s_dve, 1))

                ln_scope = {"psR": psP, "psAB": psP, "x2": lx2, "rows": lrows,
                            "tscr": ltscr, "rstride": 128, "x2w": 128}

                for apar_ in range(2):
                    P.op("vector", "memset",
                         psAp[64:128, apar_ * 512: apar_ * 512 + 64], 0.0)

                t_Strue = None
                t_psS_read = None
                prev_epi = None
                prev_sel = [None, None]
                t_wo_fl = None
                ps_ev = {0: None, 1: None}
                t_vtm = None

                for tc in range(cfg.n_tc):
                    if tc == 0:
                        P.waits("tensor", t_prev, t_wc, t_revp, t_ptlast,
                                *aw_toks)
                        ln_scope["t_pe_pre"] = t_prev
                    else:
                        ln_scope["t_pe_pre"] = None
                    P.waits("tensor", ps_ev[0], ps_ev[1])
                    t_xx = emit_ln_apply(ln_scope, (tc * 128, 128), xx, 128, 0,
                                         True)
                    P.wait("tensor", t_xx)
                    for wt, dst in ((wk_sb, Ktm), (wv_sb, Vtm)):
                        for half in range(2):
                            P.wait("tensor", ps_ev[half])
                            for dc in range(DC):
                                tmm = P.op("tensor", "matmul",
                                           psP[:, half * 512:(half + 1) * 512],
                                           xx[:, ts(dc, 128)],
                                           wt[:, dc * D + half * 512:
                                              dc * D + (half + 1) * 512],
                                           start=(dc == 0), stop=(dc == DC - 1),
                                           inc=(s_pe, 1) if dc == DC - 1 else None)
                            P.wait("scalar", tmm)
                            ps_ev[half] = P.op("scalar", "copy",
                                               dst[:, half * 512:(half + 1) * 512],
                                               psP[:, half * 512:(half + 1) * 512],
                                               inc=(s_act, 1))
                        if dst is Vtm:
                            t_vtm = ps_ev[1]
                    cnt = 0
                    for wt, dst, fn in ((wk_sb, Kfm, AF.Identity),
                                        (wv_sb, Vfm, AF.Identity),
                                        (wr_sb, Rfm, AF.Identity),
                                        (wgg_sb, Gfm, AF.Silu)):
                        for ec in range(DC):
                            half = cnt % 2
                            cnt += 1
                            P.wait("tensor", ps_ev[half])
                            for dc in range(DC):
                                tmm = P.op("tensor", "matmul",
                                           psP[:, half * 512: half * 512 + 128],
                                           wt[:, dc * D + ec * 128:
                                              dc * D + (ec + 1) * 128],
                                           xx[:, ts(dc, 128)],
                                           start=(dc == 0), stop=(dc == DC - 1),
                                           inc=(s_pe, 1) if dc == DC - 1 else None)
                            P.wait("scalar", tmm)
                            ps_ev[half] = P.op("scalar", "activation",
                                               out=dst[:, ts(ec, 128)],
                                               in_=psP[:, half * 512:
                                                       half * 512 + 128],
                                               func=fn, inc=(s_act, 1))
                    t_proj_done = ps_ev[(cnt - 1) % 2]
                    t_proj_done2 = ps_ev[cnt % 2]
                    P.waits("vector", t_proj_done, t_proj_done2)
                    t_k2 = P.op("vector", "tensor_tensor", k2[:], Ktm[:],
                                REVP[:], ALU.mult, inc=(s_dve, 1))
                    P.waits("tensor", t_k2, t_vtm, t_psS_read)
                    for h in range(H):
                        base = (h % 2) * 64
                        tmm = P.op("tensor", "matmul",
                                   psS[base:base + 64, ts(h // 2, 64)],
                                   k2[:, h * HD:(h + 1) * HD],
                                   Vtm[:, h * HD:(h + 1) * HD],
                                   start=True, stop=True,
                                   inc=(s_pe, 1) if h == H - 1 else None)
                    t_sadd = tmm
                    if tc == 0:
                        P.wait("vector", t_sadd)
                        P.op("vector", "tensor_copy", S_true[:], psS[:])
                        t_Strue = P.op("vector", "tensor_copy", S_bf[:],
                                       S_true[:], inc=(s_dve, 1))
                        t_psS_read = t_Strue
                        continue
                    kv3 = lambda t_: t_[:].rearrange("q (c t) -> q c t", t=128)
                    P.op("vector", "tensor_tensor", khatL[:], Kfm[:],
                         PThL[:], ALU.mult)
                    P.op("vector", "tensor_tensor", ktilL[:], Kfm[:],
                         PTtL[:], ALU.mult)
                    P.op("vector", "tensor_tensor",
                         ktilB[:].rearrange("q (c t) -> q c t", t=64),
                         kv3(Kfm)[:, :, 0:64],
                         PTtB[:].rearrange("q (c t) -> q c t", t=64), ALU.mult)
                    t_con = P.op("vector", "tensor_tensor", qt[:], Kfm[:],
                                 PTq[:], ALU.mult, inc=(s_dve, 1))
                    P.waits("tensor", t_con, prev_epi, t_Strue)
                    t_sel = dict()
                    for h in range(H):
                        base = (h % 2) * 64
                        ec = h // 2
                        apar = h % 2
                        c0 = apar * 512
                        P.wait("tensor", t_sel.get(h - 2, prev_sel[apar]))
                        P.op("tensor", "matmul",
                             psAp[0:64, c0:c0 + 64],
                             ktilL[base:base + 64, ec * 128: ec * 128 + 64],
                             khatL[base:base + 64, ec * 128: ec * 128 + 64],
                             start=True, stop=True)
                        P.op("tensor", "matmul",
                             psAp[0:64, c0 + 64:c0 + 128],
                             ktilB[base:base + 64, ts(ec, 64)],
                             khatL[base:base + 64,
                                   ec * 128 + 64: ec * 128 + 128],
                             start=True, stop=True)
                        t_amm = P.op("tensor", "matmul",
                                     psAp[64:128, c0 + 64:c0 + 128],
                                     ktilL[base:base + 64,
                                           ec * 128 + 64: ec * 128 + 128],
                                     khatL[base:base + 64,
                                           ec * 128 + 64: ec * 128 + 128],
                                     start=True, stop=True, inc=(s_pe, 1))
                        P.wait("vector", t_amm)
                        t_sel[h] = P.op("vector", "select", Am[:, ts(apar, 128)],
                                        tri_u8[:],
                                        psAp[:, apar * 512: apar * 512 + 128],
                                        zeros_bf[:], inc=(s_dve, 1))
                        P.wait("tensor", t_sel[h])
                        P.op("tensor", "matmul",
                             psO[base:base + 64, ts(ec, 128)],
                             Vtm[:, h * HD:(h + 1) * HD], Am[:, ts(apar, 128)],
                             start=True, stop=False)
                        t_out = P.op("tensor", "matmul",
                                     psO[base:base + 64, ts(ec, 128)],
                                     S_bf[base:base + 64, ts(ec, 64)],
                                     qt[base:base + 64, ts(ec, 128)],
                                     start=False, stop=True,
                                     inc=(s_pe, 1) if h == H - 1 else None)
                    prev_sel = [t_sel[H - 2], t_sel[H - 1]]
                    t_outs = t_out
                    P.wait("vector", t_outs)
                    P.op("vector", "tensor_tensor", S_true[:], S_true[:],
                         WCf[:], ALU.mult)
                    t_Strue = P.op("vector", "tensor_tensor", S_true[:],
                                   S_true[:], psS[:], ALU.add, inc=(s_dve, 1))
                    t_psS_read = t_Strue
                    t_Strue = P.op("vector", "tensor_copy", S_bf[:], S_true[:],
                                   inc=(s_dve, 1))
                    P.op("vector", "tensor_scalar", psO[:, 0:1024],
                         psO[:, 0:1024], 2.0 ** 60, None, ALU.mult)
                    P.op("vector", "tensor_tensor", xx[:], Kfm[:], Vfm[:],
                         ALU.mult)
                    P.op("vector", "tensor_tensor", xx[:], xx[:], tf_full[:],
                         ALU.mult)
                    P.op("vector", "tensor_tensor", psO[:, 0:1024],
                         psO[:, 0:1024], xx[:], ALU.add)
                    P.op("vector", "tensor_tensor", psO[:, 0:1024],
                         psO[:, 0:1024], Rfm[:], ALU.mult)
                    t_w3 = P.op("vector", "tensor_tensor", w3[:],
                                psO[:, 0:1024], Gfm[:], ALU.mult,
                                inc=(s_dve, 1))
                    if "at1" in cfg.debug and tc == 1:
                        toks = []
                        P.waits("sync", t_w3, t_Strue)
                        for nm_, tile_, dt_ in (
                                ("at_xx", xx, BF16), ("at_Ktm", Ktm, BF16),
                                ("at_Vtm", Vtm, BF16), ("at_Kfm", Kfm, BF16),
                                ("at_Vfm", Vfm, BF16), ("at_Rfm", Rfm, BF16),
                                ("at_Gfm", Gfm, BF16), ("at_khat", khatL, BF16),
                                ("at_ktil", ktilL, BF16), ("at_qt", qt, BF16),
                                ("at_w3", w3, BF16), ("at_Sbf", S_bf, BF16),
                                ("at_k2", k2, BF16), ("at_Am", Am, BF16),
                                ("at_PTh", PThL, F32), ("at_REVP", REVP, F32),
                                ("at_WCf", WCf, F32)):
                            d_ = dbg(nm_, list(tile_[:].shape), dt_)
                            if d_ is not None:
                                tk = P.op("sync", "dma_start", out=d_[:, :],
                                          in_=tile_[:], inc=(s_st, 16))
                                toks.append(tk)
                        if toks:
                            P.wait("vector", toks[-1])
                            P.wait("tensor", toks[-1])
                    P.wait("tensor", t_w3)
                    for eo in range(DC):
                        for dc in range(DC):
                            tmm = P.op("tensor", "matmul",
                                       psO[:, eo * 128:(eo + 1) * 128],
                                       wo_sb[:, dc * D + eo * 128:
                                             dc * D + (eo + 1) * 128],
                                       w3[:, ts(dc, 128)],
                                       start=(dc == 0), stop=(dc == DC - 1),
                                       inc=(s_pe, 1)
                                       if (eo == DC - 1 and dc == DC - 1)
                                       else None)
                    P.wait("vector", tmm)
                    xv = x_sb[:].rearrange("q (c t) -> q c t", t=T)[
                        :, :, tc * 128:(tc + 1) * 128]
                    t_wo_fl = P.op("vector", "tensor_tensor", xv, xv,
                                   psO[:, 0:1024].rearrange(
                                       "q (c t) -> q c t", t=128),
                                   ALU.add, inc=(s_dve, 1))
                    prev_epi = t_wo_fl
                d = dbg("x_after_2", [128, DC * T])
                if d is not None:
                    P.wait("sync", t_wo_fl)
                    P.op("sync", "dma_start", out=d[:, :], in_=x_sb[:],
                         inc=(s_st, 16))
                return t_wo_fl

        tdone = None
        if "1" in cfg.phases:
            tdone = emit_ffn(1, tdone)
            P.flush(block)
        if "2" in cfg.phases:
            tdone = emit_attn(tdone)
            P.flush(block)
        if "3" in cfg.phases:
            tdone = emit_ffn(3, tdone)
        P.wait("sync", tdone)
        P.op("sync", "dma_start",
             out=ap["y"].rearrange("(c p) t -> p c t", p=128),
             in_=x_sb[:].rearrange("p (c t) -> p c t", t=T)[:, :, W:T],
             inc=(s_st, 16))
        P.wait("sync", (s_st, s_st.count))
        P.flush(block)

    return nc, list(dbg_outs)



def prep_host(cfg: Cfg, W_attn, ffn, td, tf, tm, gate_w, gate_bias):
    import ml_dtypes
    f = np.ascontiguousarray
    Wg_e, Wu_e, Wd_e, Sg, Su, Sd = [np.asarray(a, np.float32) for a in ffn]
    D = cfg.D
    gateT = f(np.concatenate([Wg_e.transpose(2, 0, 1).reshape(D, -1), Sg.T], axis=1))
    upT = f(np.concatenate([Wu_e.transpose(2, 0, 1).reshape(D, -1), Su.T], axis=1))
    downT = f(np.concatenate([Wd_e.transpose(0, 2, 1).reshape(-1, D), Sd.T], axis=0))
    assert gateT.shape == (D, cfg.NI) and downT.shape == (cfg.NI, D)
    eid = np.empty(cfg.NI, np.int64)
    eid[:cfg.NE * cfg.I_exp] = np.arange(cfg.NE * cfg.I_exp) // cfg.I_exp
    eid[cfg.NE * cfg.I_exp:] = cfg.NE
    m1T = np.zeros((cfg.NE + 1, cfg.NI), np.float32)
    m1T[eid, np.arange(cfg.NI)] = 1.0
    m1T = f(m1T.astype(ml_dtypes.bfloat16))

    td = np.asarray(td, np.float32).reshape(cfg.H, cfg.HD)
    tf = np.asarray(tf, np.float32).reshape(cfg.H, cfg.HD)
    tm = np.asarray(tm, np.float32).reshape(-1)
    td_s = f(td.reshape(cfg.H // 2, 2, cfg.HD).transpose(1, 2, 0).reshape(128, -1))
    tf_s = f(tf.reshape(cfg.H // 2, 2, cfg.HD).transpose(1, 2, 0).reshape(128, -1))
    tm_s = f(tm.reshape(cfg.DC, 128).T)
    Wk, Wv, Wr, Wgt, Wo = [np.asarray(a, np.float32) for a in W_attn]
    return {
        "gateT": gateT, "upT": upT, "downT": downT, "m1T": m1T,
        "wkT": f(Wk.T), "wvT": f(Wv.T), "wrT": f(Wr.T), "wgT": f(Wgt.T),
        "woT": f(Wo.T),
        "gwT": f(np.asarray(gate_w, np.float32).T),
        "td_s": td_s, "tf_s": tf_s, "tm_s": tm_s,
        "td_row": f(td.reshape(1, -1)),
        "gb_row": f(np.asarray(gate_bias, np.float32).reshape(1, -1)),
    }


def run_cores(cfg: Cfg, xT_full, host, **kw):
    nc, dbg_names = build_program(cfg)
    in_maps = []
    for c in range(cfg.n_cores):
        t0 = c * cfg.T_main
        halo = (np.zeros((cfg.D, cfg.W), np.float32) if c == 0
                else np.ascontiguousarray(xT_full[:, t0 - cfg.W:t0]))
        xc = np.concatenate([halo, xT_full[:, t0:t0 + cfg.T_main]], axis=1)
        m = {"xT": np.ascontiguousarray(xc)}
        m.update(host)
        in_maps.append(m)
    res = run_bass_kernel_spmd(nc, in_maps, core_ids=list(range(cfg.n_cores)),
                               **kw)
    return res


def kernel(**inputs):
    cfg = Cfg()
    x = np.asarray(inputs["L0_attn_x"], np.float32)
    Ttot = x.shape[1]
    assert Ttot == cfg.T_main * cfg.n_cores
    xT = np.ascontiguousarray(x[0].T)
    host = prep_host(
        cfg,
        (inputs["L0_attn_Wk"], inputs["L0_attn_Wv"], inputs["L0_attn_Wr"],
         inputs["L0_attn_Wg"], inputs["L0_attn_Wo"]),
        (inputs["L0_ffn_Wg"], inputs["L0_ffn_Wu"], inputs["L0_ffn_Wd"],
         inputs["L0_ffn_Sg"], inputs["L0_ffn_Su"], inputs["L0_ffn_Sd"]),
        inputs["L0_attn_time_decay"], inputs["L0_attn_time_first"],
        inputs["L0_attn_time_mix"], inputs["L0_ffn_gate_w"],
        inputs["L0_ffn_gate_bias"])
    res = run_cores(cfg, xT, host)
    y = np.concatenate([res.results[c]["y"] for c in range(cfg.n_cores)], axis=1)
    return np.ascontiguousarray(y.T).reshape(1, Ttot, cfg.D).astype(np.float32)


if __name__ == "__main__":
    cfg = Cfg()
    nc, _ = build_program(cfg)
    print("instructions:", sum(1 for _ in nc.all_instructions()))


# version 12 (final)
# speedup vs baseline: 1.1087x; 1.0726x over previous
import numpy as np
from contextlib import ExitStack

import concourse.bass as bass
import concourse.mybir as mybir
from concourse.bass_utils import run_bass_kernel_spmd

F32 = mybir.dt.float32
BF16 = mybir.dt.bfloat16
I8 = mybir.dt.int8
AF = mybir.ActivationFunctionType
ALU = mybir.AluOpType
AX = mybir.AxisListType

ENGINES = ("tensor", "vector", "scalar", "gpsimd", "sync")


class Cfg:
    def __init__(self, T_main=2048, W=128, NE=64, I_exp=704, SI=1408,
                 NT=512, GROUP=5, n_cores=8, debug=(), phases="123"):
        self.phases = phases
        self.T_main = T_main
        self.W = W
        self.T = T_main + W
        self.NE = NE
        self.I_exp = I_exp
        self.SI = SI
        self.NI = NE * I_exp + SI
        assert self.NI % 128 == 0
        self.IC = self.NI // 128
        self.NT = NT
        self.GROUP = GROUP
        self.D = 1024
        self.DC = 8
        self.H = 16
        self.HD = 64
        self.n_cores = n_cores
        self.debug = tuple(debug)
        assert T_main % NT == 0 and W % 128 == 0
        self.t_tiles_1 = ([(0, W)] if W else []) + \
            [(W + i * NT, NT) for i in range(T_main // NT)]
        self.t_tiles_3 = [(W + i * NT, NT) for i in range(T_main // NT)]
        self.n_tc = self.T // 128
        self.groups = []
        c = 0
        while c < self.IC:
            n = min(GROUP, self.IC - c)
            self.groups.append((c, n))
            c += n


class Sem:
    __slots__ = ("name", "h", "count")

    def __init__(self, name, h):
        self.name, self.h, self.count = name, h, 0


GP_COMPUTE = {"memset", "iota", "affine_select", "tensor_copy",
              "tensor_tensor", "tensor_scalar", "tensor_reduce"}


class Prog:

    def __init__(self, nc):
        self.nc = nc
        self.q = {e: [] for e in ENGINES}
        self.last_wait = {e: {} for e in ENGINES}
        self.auto = {}
        self.last_comp = {e: None for e in ENGINES}

    def set_auto(self, engine, sem):
        self.auto[engine] = sem

    def op(self, engine, method, *args, inc=None, **kwargs):
        chain = (engine in ("vector", "scalar")
                 or (engine == "gpsimd" and method in GP_COMPUTE))
        if chain and inc is None and engine in self.auto:
            inc = (self.auto[engine], 1)
        if chain:
            self.wait(engine, self.last_comp[engine])
        tok = None
        if inc is not None:
            sem, amt = inc
            sem.count += amt
            tok = (sem, sem.count)

        def fn(eng, method=method, args=args, kwargs=kwargs, inc=inc):
            inst = getattr(eng, method)(*args, **kwargs)
            if inc is not None:
                inst.then_inc(inc[0].h, inc[1])
        self.q[engine].append(fn)
        if chain:
            self.last_comp[engine] = tok
        return tok

    def wait(self, engine, tok):
        if tok is None:
            return
        sem, val = tok
        if self.last_wait[engine].get(sem.name, 0) >= val:
            return
        self.last_wait[engine][sem.name] = val

        def fn(eng, sem=sem, val=val):
            eng.wait_ge(sem.h, val)
        self.q[engine].append(fn)

    def waits(self, engine, *toks):
        for t in toks:
            self.wait(engine, t)

    def flush(self, block):
        for e in ENGINES:
            ops = self.q[e]
            if not ops:
                continue
            self.q[e] = []

            def runner(eng, ops=ops):
                for f in ops:
                    f(eng)
            getattr(block, e)(runner)


def ts(i, n):
    return slice(i * n, (i + 1) * n)



def build_program(cfg: Cfg):
    nc = bass.Bass("TRN2", target_bir_lowering=False)
    D, DC, NT, W, T = cfg.D, cfg.DC, cfg.NT, cfg.W, cfg.T
    NI = cfg.NI
    NEx = cfg.NE + 1
    H, HD = cfg.H, cfg.HD
    LNEPS = 1e-5

    ap = {}

    def din(name, shape, dt=F32):
        ap[name] = nc.dram_tensor(name, shape, dt, kind="ExternalInput").ap()

    din("xT", [D, T])
    din("gateT", [D, NI])
    din("upT", [D, NI])
    din("downT", [NI, D])
    din("m1T", [NEx, NI], BF16)
    for nm in ("wkT", "wvT", "wrT", "wgT", "woT"):
        din(nm, [D, D])
    din("gwT", [D, cfg.NE])
    din("td_s", [128, 8])
    din("tf_s", [128, 8])
    din("tm_s", [128, 8])
    din("td_row", [1, D])
    din("gb_row", [1, cfg.NE])
    ap["y"] = nc.dram_tensor("y", [D, cfg.T_main], F32, kind="ExternalOutput").ap()

    dbg_outs = {}

    def dbg(name, shape, dt=F32):
        if name in cfg.debug:
            if name not in dbg_outs:
                dbg_outs[name] = nc.dram_tensor("dbg_" + name, shape, dt,
                                                kind="ExternalOutput").ap()
            return dbg_outs[name]
        return None

    P = Prog(nc)

    with ExitStack() as g:
        sbuf = lambda n, s, dt=F32: g.enter_context(nc.sbuf_tensor(n, s, dt))
        sem = lambda n: Sem(n, g.enter_context(nc.semaphore(n)))

        x_sb = sbuf("x_sb", [128, DC * T])
        ident_bf = sbuf("ident_bf", [128, 128], BF16)
        tri_u8 = sbuf("tri_u8", [128, 128], I8)
        zeros_bf = sbuf("zeros_bf", [128, 128], BF16)
        ones_col = sbuf("ones_col", [128, 1])
        ones_row = sbuf("ones_row", [1, 128])
        td_s = sbuf("td_s_sb", [128, 8])
        tf_s = sbuf("tf_s_sb", [128, 8])
        tm_s = sbuf("tm_s_sb", [128, 8])
        td_row = sbuf("td_row_sb", [1, D])
        gb_sb = sbuf("gb_sb", [1, cfg.NE])
        gbrep = sbuf("gbrep", [128, cfg.NE])
        gwT_sb = sbuf("gwT_sb", [128, DC * cfg.NE], BF16)
        gscr = sbuf("gscr", [128, 128])

        s_ld = sem("s_ld")
        s_wg = [sem("s_wg0"), sem("s_wg1")]
        s_wu = [sem("s_wu0"), sem("s_wu1")]
        s_wd = [sem("s_wd0"), sem("s_wd1")]
        s_m = [sem("s_m0"), sem("s_m1")]
        s_wa = sem("s_wa")
        s_pe = sem("s_pe")
        s_act = sem("s_act")
        s_dve = sem("s_dve")
        s_gp = sem("s_gp")
        s_st = sem("s_st")
        P.set_auto("vector", s_dve)
        P.set_auto("scalar", s_act)
        P.set_auto("gpsimd", s_gp)

        block = g.enter_context(nc.Block())

        P.op("sync", "dma_start",
             out=x_sb[:].rearrange("p (c t) -> p c t", t=T),
             in_=ap["xT"].rearrange("(c p) t -> p c t", p=128),
             inc=(s_ld, 16))
        for name, tile in (("td_s", td_s), ("tf_s", tf_s), ("tm_s", tm_s),
                           ("td_row", td_row), ("gb_row", gb_sb)):
            P.op("sync", "dma_start", out=tile[:], in_=ap[name][:, :],
                 inc=(s_ld, 16))
        t_small = (s_ld, s_ld.count)
        t_x = t_small
        t_gwT = P.op("gpsimd", "dma_start",
                     out=gwT_sb[:].rearrange("p (c e) -> p c e", e=cfg.NE),
                     in_=ap["gwT"].rearrange("(c p) e -> p c e", p=128),
                     inc=(s_wa, 16))

        P.op("gpsimd", "memset", ident_bf[:], 0.0)
        P.op("gpsimd", "affine_select", out=ident_bf[:], in_=ident_bf[:],
             compare_op=ALU.not_equal, fill=1.0, base=0,
             pattern=[[-1, 128]], channel_multiplier=1)
        P.op("gpsimd", "memset", zeros_bf[:], 0.0)
        P.op("gpsimd", "memset", gscr[:], 0.0)
        t_trif = P.op("gpsimd", "affine_select", out=gscr[:], in_=gscr[:],
                      compare_op=ALU.is_gt, fill=1.0, base=0,
                      pattern=[[-1, 128]], channel_multiplier=1)
        P.op("gpsimd", "memset", ones_col[:], 1.0)
        t_gp_const = P.op("gpsimd", "memset", ones_row[:], 1.0, inc=(s_gp, 1))
        P.wait("vector", t_gp_const)
        t_tri = P.op("vector", "tensor_copy", tri_u8[:], gscr[:])

        with nc.psum_tensor("ps_init", [128, 512], F32) as ps_init:
            P.waits("tensor", t_small, t_gp_const)
            t_mm = P.op("tensor", "matmul", ps_init[:, 0:cfg.NE], ones_row[:],
                        gb_sb[:], start=True, stop=True, inc=(s_pe, 1))
            P.wait("scalar", t_mm)
            t_gbrep = P.op("scalar", "copy", gbrep[:], ps_init[:, 0:cfg.NE],
                           inc=(s_act, 1))
            P.wait("tensor", t_gbrep)

        def emit_ln_apply(scope, x_cols, out_tile, out_w, out_off, with_tm):
            psR, psAB = scope["psR"], scope["psAB"]
            x2, rows, tscr = scope["x2"], scope["rows"], scope["tscr"]
            rs = scope.get("rstride", 512)
            xw = scope.get("x2w", 512)
            t0, tlen = x_cols
            last = None
            nblk = (tlen + NT - 1) // NT
            tm2_toks = {}
            for bi in range(nblk):
                b0 = t0 + bi * NT
                blen = min(NT, tlen - bi * NT)
                ob0 = out_off + bi * NT
                for dc in range(DC):
                    xs = x_sb[:, dc * T + b0: dc * T + b0 + blen]
                    par = dc % 2
                    P.wait("vector", tm2_toks.get((bi, dc - 2)))
                    tw = P.op("vector", "tensor_tensor",
                              x2[:, par * xw:par * xw + blen], xs, xs,
                              ALU.mult, inc=(s_dve, 1))
                    if scope.get("t_pe_pre") is not None and bi == 0 and dc == 0:
                        P.wait("tensor", scope["t_pe_pre"])
                    P.op("tensor", "matmul", psR[0:1, 0:blen], ones_col[:], xs,
                         start=(dc == 0), stop=(dc == DC - 1))
                    P.wait("tensor", tw)
                    tm2_toks[(bi, dc)] = P.op(
                        "tensor", "matmul", psR[0:1, 512:512 + blen], ones_col[:],
                        x2[:, par * xw:par * xw + blen],
                        start=(dc == 0), stop=(dc == DC - 1), inc=(s_pe, 1))
                t_stats = tm2_toks[(bi, DC - 1)]
                P.wait("vector", t_stats)
                mean = rows[0:1, 0:blen]
                msq = rows[0:1, rs:rs + blen]
                Arow = rows[0:1, 2 * rs:2 * rs + blen]
                Brow = rows[0:1, 3 * rs:3 * rs + blen]
                P.op("vector", "tensor_scalar", mean, psR[0:1, 0:blen],
                     1.0 / D, None, ALU.mult)
                P.op("vector", "tensor_scalar", msq, psR[0:1, 512:512 + blen],
                     1.0 / D, None, ALU.mult)
                P.op("vector", "tensor_tensor", Arow, mean, mean, ALU.mult)
                P.op("vector", "tensor_tensor", Arow, msq, Arow, ALU.subtract)
                t_v = P.op("vector", "tensor_scalar", Arow, Arow, LNEPS, None,
                           ALU.add, inc=(s_dve, 1))
                P.wait("scalar", t_v)
                P.op("scalar", "activation", out=Arow, in_=Arow, func=AF.Ln)
                t_a = P.op("scalar", "activation", out=Arow, in_=Arow,
                           func=AF.Exp, scale=-0.5, inc=(s_act, 1))
                P.wait("vector", t_a)
                t_b = P.op("vector", "tensor_tensor", Brow, mean, Arow,
                           ALU.mult, inc=(s_dve, 1))
                P.wait("tensor", t_b)
                ta = P.op("tensor", "matmul", psAB[:, 0:blen], ones_row[:],
                          Arow, start=True, stop=True)
                tb = P.op("tensor", "matmul", psAB[:, 512:512 + blen],
                          ones_row[:], Brow, start=True, stop=True,
                          inc=(s_pe, 1))
                P.wait("vector", tb)
                for dc in range(DC):
                    xs = x_sb[:, dc * T + b0: dc * T + b0 + blen]
                    P.op("vector", "tensor_tensor", tscr[:, 0:blen], xs,
                         psAB[:, 0:blen], ALU.mult)
                    if with_tm:
                        P.op("vector", "tensor_tensor", tscr[:, 0:blen],
                             tscr[:, 0:blen], psAB[:, 512:512 + blen],
                             ALU.subtract)
                        last = P.op("vector", "tensor_scalar",
                                    out_tile[:, dc * out_w + ob0:
                                             dc * out_w + ob0 + blen],
                                    tscr[:, 0:blen], tm_s[:, dc:dc + 1], None,
                                    ALU.mult, inc=(s_dve, 1))
                    else:
                        last = P.op("vector", "tensor_tensor",
                                    out_tile[:, dc * out_w + ob0:
                                             dc * out_w + ob0 + blen],
                                    tscr[:, 0:blen], psAB[:, 512:512 + blen],
                                    ALU.subtract, inc=(s_dve, 1))
                P.wait("tensor", last)
            return last

        def emit_ffn(call, t_prev):
            t_tiles = cfg.t_tiles_1 if call == 1 else cfg.t_tiles_3
            TW = T
            with ExitStack() as fs:
                fsb = lambda n, s, dt=F32: fs.enter_context(
                    nc.sbuf_tensor(f"{n}_c{call}", s, dt))
                xln = fsb("xln", [128, DC * TW], BF16)
                rw_fm = fsb("rw_fm", [128, TW], BF16)
                GW = cfg.GROUP * 128
                wg_b = fsb("wg_b", [128, 2 * DC * GW], BF16)
                wu_b = fsb("wu_b", [128, 2 * DC * GW], BF16)
                wd_b = fsb("wd_b", [128, 2 * cfg.GROUP * D], BF16)
                m1_b = fsb("m1_b", [NEx, 2 * GW], BF16)
                h_sb = fsb("h_sb", [128, 2 * cfg.GROUP * NT], BF16)
                sS = fsb("sS", [128, 2 * NT], BF16)
                t1 = fsb("t1", [128, 2 * NT])


                with ExitStack() as lns:
                    lpsR = lns.enter_context(
                        nc.psum_tensor(f"lnpsR_{call}", [1, 1024], F32))
                    lpsAB = lns.enter_context(
                        nc.psum_tensor(f"lnpsAB_{call}", [128, 1024], F32))
                    lx2 = lns.enter_context(
                        nc.sbuf_tensor(f"lx2_{call}", [128, 1024], F32))
                    lrows = lns.enter_context(
                        nc.sbuf_tensor(f"lrows_{call}", [1, 2048], F32))
                    ltscr = lns.enter_context(
                        nc.sbuf_tensor(f"ltscr_{call}", [128, 512], F32))
                    scope = {"psR": lpsR, "psAB": lpsAB, "x2": lx2,
                             "rows": lrows, "tscr": ltscr, "t_pe_pre": t_prev}
                    if call == 1:
                        P.wait("vector", t_x)
                        t_ln = emit_ln_apply(scope, (0, T), xln, TW, 0, False)
                    else:
                        t_ln = emit_ln_apply(scope, (W, cfg.T_main), xln, TW, W,
                                             False)

                    psRt = lns.enter_context(
                        nc.psum_tensor(f"rpsR_{call}", [128, 1024], F32))
                    psTp = lns.enter_context(
                        nc.psum_tensor(f"rpsT_{call}", [128, 128], BF16))
                    sc = lns.enter_context(
                        nc.sbuf_tensor(f"rsc_{call}", [128, 2 * 64], F32))
                    s4c = lns.enter_context(
                        nc.sbuf_tensor(f"rs4c_{call}", [128, 64], F32))
                    gsv = lns.enter_context(
                        nc.sbuf_tensor(f"rgs_{call}", [128, 8], F32))
                    m8 = lns.enter_context(
                        nc.sbuf_tensor(f"rm8_{call}", [128, 8], F32))
                    msk = lns.enter_context(
                        nc.sbuf_tensor(f"rmsk_{call}", [128, 64], F32))
                    gm = lns.enter_context(
                        nc.sbuf_tensor(f"rgm_{call}", [128, 8], F32))
                    rww = lns.enter_context(
                        nc.sbuf_tensor(f"rww_{call}", [128, 64], F32))
                    den = lns.enter_context(
                        nc.sbuf_tensor(f"rden_{call}", [128, 1], F32))
                    rwtm = lns.enter_context(
                        nc.sbuf_tensor(f"rwtm_{call}", [128, 64], BF16))

                    tc_range = list(range(cfg.n_tc)) if call == 1 else \
                        list(range(1, cfg.n_tc))
                    P.op("vector", "memset", rw_fm[64:65, 0:TW], 0.5)
                    P.wait("tensor", t_ln)
                    P.waits("tensor", t_gwT)
                    sc_tok = {0: None, 1: None}
                    prev_evac = None
                    prev_dve = None
                    for tc in tc_range:
                        par = tc % 2
                        P.wait("tensor", sc_tok[par])
                        for dc in range(DC):
                            tmm = P.op("tensor", "matmul",
                                       psRt[:, par * 512:par * 512 + cfg.NE],
                                       xln[:, dc * TW + tc * 128:
                                           dc * TW + (tc + 1) * 128],
                                       gwT_sb[:, ts(dc, cfg.NE)],
                                       start=(dc == 0), stop=(dc == DC - 1),
                                       inc=(s_pe, 1) if dc == DC - 1 else None)
                        P.wait("scalar", tmm)
                        P.wait("scalar", prev_dve)
                        sc_tok[par] = P.op("scalar", "activation",
                                           out=sc[:, ts(par, 64)],
                                           in_=psRt[:, par * 512:par * 512 + cfg.NE],
                                           func=AF.Sigmoid, inc=(s_act, 1))
                        P.wait("vector", sc_tok[par])
                        scv = sc[:, ts(par, 64)]
                        P.op("vector", "tensor_tensor", s4c[:], scv, gbrep[:],
                             ALU.add)
                        for gi in range(8):
                            P.op("vector", "max", out=m8[:], in_=s4c[:, ts(gi, 8)])
                            P.op("vector", "tensor_tensor", gsv[:, gi:gi + 1],
                                 m8[:, 0:1], m8[:, 1:2], ALU.add)
                        P.op("vector", "max", out=m8[:], in_=gsv[:])
                        P.op("vector", "tensor_scalar", gm[:], gsv[:],
                             m8[:, 3:4], None, ALU.is_ge)
                        for gi in range(8):
                            P.op("vector", "tensor_scalar", msk[:, ts(gi, 8)],
                                 s4c[:, ts(gi, 8)], gm[:, gi:gi + 1], None,
                                 ALU.mult)
                        P.op("vector", "max", out=m8[:], in_=msk[:])
                        P.op("vector", "tensor_scalar", msk[:], msk[:],
                             m8[:, 7:8], None, ALU.is_ge)
                        P.op("vector", "tensor_tensor", rww[:], scv, msk[:],
                             ALU.mult)
                        P.op("vector", "tensor_reduce", den[:], rww[:],
                             axis=AX.X, op=ALU.add)
                        P.op("vector", "tensor_scalar", den[:], den[:], 1e-20,
                             None, ALU.add)
                        P.op("vector", "reciprocal", den[:], den[:])
                        P.op("vector", "tensor_scalar", den[:], den[:], 1.25,
                             None, ALU.mult)
                        t_rwtm = P.op("vector", "tensor_scalar", rwtm[:],
                                      rww[:], den[:], None, ALU.mult,
                                      inc=(s_dve, 1))
                        prev_dve = t_rwtm
                        P.wait("tensor", t_rwtm)
                        P.wait("tensor", prev_evac)
                        t_tp = P.op("tensor", "transpose", psTp[0:64, :],
                                    rwtm[:], ident_bf[:], inc=(s_pe, 1))
                        P.wait("vector", t_tp)
                        prev_evac = P.op("vector", "tensor_copy",
                                         rw_fm[0:64, tc * 128:(tc + 1) * 128],
                                         psTp[0:64, :], inc=(s_dve, 1))
                    t_rw_ready = prev_evac
                    if call == 1:
                        for nm_, tile_, shape_, dt_ in (
                                ("rt_sc", sc, [128, 128], F32),
                                ("rt_s4c", s4c, [128, 64], F32),
                                ("rt_msk", msk, [128, 64], F32),
                                ("rt_rww", rww, [128, 64], F32),
                                ("rt_den", den, [128, 1], F32),
                                ("rt_m8", m8, [128, 8], F32),
                                ("rt_gm", gm, [128, 8], F32),
                                ("rt_gsv", gsv, [128, 8], F32),
                                ("rt_rwtm", rwtm, [128, 64], BF16),
                                ("rt_gbrep", gbrep, [128, 64], F32)):
                            d_ = dbg(nm_, shape_, dt_)
                            if d_ is not None:
                                P.wait("sync", t_rw_ready)
                                P.op("sync", "dma_start", out=d_[:, :],
                                     in_=tile_[:], inc=(s_st, 16))
                    d = dbg(f"rw_{call}", [128, TW], BF16)
                    if d is not None:
                        P.wait("sync", t_rw_ready)
                        P.op("sync", "dma_start", out=d[:, :], in_=rw_fm[:],
                             inc=(s_st, 16))
                    d = dbg(f"xln_{call}", [128, DC * TW], BF16)
                    if d is not None:
                        P.wait("sync", t_ln)
                        P.op("sync", "dma_start", out=d[:, :], in_=xln[:],
                             inc=(s_st, 16))

                psA = fs.enter_context(nc.psum_tensor(f"psA_{call}", [128, 1024], F32))
                psB = fs.enter_context(nc.psum_tensor(f"psB_{call}", [128, 1024], F32))
                psC = fs.enter_context(nc.psum_tensor(f"psC_{call}", [128, 1024], F32))
                psD = fs.enter_context(nc.psum_tensor(f"psD_{call}", [128, 1024], F32))

                gateT_v = ap["gateT"].rearrange("(c p) i -> p c i", p=128)
                upT_v = ap["upT"].rearrange("(c p) i -> p c i", p=128)
                downT_v = ap["downT"].rearrange("(g p) d -> p g d", p=128)

                w_tok = {}
                free_tok = {0: t_prev, 1: t_prev}

                def emit_group_piece(gi, piece, gate_tok):
                    c0, ncg = cfg.groups[gi]
                    par = gi % 2
                    i0, wlen = c0 * 128, ncg * 128
                    P.wait("gpsimd", gate_tok)
                    if piece == 0:
                        P.wait("gpsimd", free_tok[par])
                        P.op("gpsimd", "dma_start",
                             out=wg_b[:, par * DC * GW: (par + 1) * DC * GW]
                             .rearrange("p (c i) -> p c i", i=GW)[:, :, 0:wlen],
                             in_=gateT_v[:, :, i0:i0 + wlen],
                             inc=(s_wg[par], 16))
                        P.wait("sync", free_tok[par])
                        P.wait("sync", gate_tok)
                        P.op("sync", "dma_start",
                             out=m1_b[:, par * GW: par * GW + wlen],
                             in_=ap["m1T"][:, i0:i0 + wlen],
                             inc=(s_m[par], 16))
                    elif piece == 1:
                        P.op("gpsimd", "dma_start",
                             out=wu_b[:, par * DC * GW: (par + 1) * DC * GW]
                             .rearrange("p (c i) -> p c i", i=GW)[:, :, 0:wlen],
                             in_=upT_v[:, :, i0:i0 + wlen],
                             inc=(s_wu[par], 16))
                    else:
                        P.op("gpsimd", "dma_start",
                             out=wd_b[:, par * cfg.GROUP * D:
                                      par * cfg.GROUP * D + ncg * D]
                             .rearrange("p (c d) -> p c d", d=D),
                             in_=downT_v[:, c0:c0 + ncg, :],
                             inc=(s_wd[par], 16))
                    w_tok[gi] = tuple((s_, s_.count) for s_ in
                                      (s_wg[par], s_wu[par], s_wd[par],
                                       s_m[par]))

                def issue_group_dma(gi):
                    for pc in range(3):
                        emit_group_piece(gi, pc, None)

                issue_group_dma(0)
                if len(cfg.groups) > 1:
                    issue_group_dma(1)
                pending = []

                silu_tok = [None, None]
                h_tok = [None, None]
                flush_tok = [None, None]
                psC_last = [None, None]
                sS_last = [None, None]
                h_guard = [None, None]
                t_first_wait_done = False
                t_fl = None

                for gi, (c0, ncg) in enumerate(cfg.groups):
                    par = gi % 2
                    gbase_g = par * DC * GW
                    gbase_d = par * cfg.GROUP * D
                    gbase_m = par * GW
                    gbase_h = par * cfg.GROUP * NT
                    P.waits("tensor", *w_tok[gi])
                    for (tt0, tlen) in t_tiles:
                        for ci in range(ncg):
                            cpar = ci % 2
                            if not t_first_wait_done:
                                P.wait("tensor", t_rw_ready)
                                t_first_wait_done = True
                            P.wait("tensor", psC_last[cpar])
                            t_rwmm = P.op("tensor", "matmul",
                                          psC[:, cpar * 512: cpar * 512 + tlen],
                                          m1_b[:, gbase_m + ci * 128:
                                               gbase_m + (ci + 1) * 128],
                                          rw_fm[0:NEx, tt0:tt0 + tlen],
                                          start=True, stop=True, inc=(s_pe, 1))
                            P.wait("tensor", silu_tok[cpar])
                            for dc in range(DC):
                                t_gmm = P.op("tensor", "matmul",
                                             psA[:, cpar * 512: cpar * 512 + tlen],
                                             wg_b[:, gbase_g + dc * GW + ci * 128:
                                                  gbase_g + dc * GW + (ci + 1) * 128],
                                             xln[:, dc * TW + tt0:
                                                 dc * TW + tt0 + tlen],
                                             start=(dc == 0), stop=(dc == DC - 1),
                                             inc=(s_pe, 1) if dc == DC - 1 else None)
                            P.wait("tensor", h_tok[cpar])
                            for dc in range(DC):
                                t_umm = P.op("tensor", "matmul",
                                             psB[:, cpar * 512: cpar * 512 + tlen],
                                             wu_b[:, gbase_g + dc * GW + ci * 128:
                                                  gbase_g + dc * GW + (ci + 1) * 128],
                                             xln[:, dc * TW + tt0:
                                                 dc * TW + tt0 + tlen],
                                             start=(dc == 0), stop=(dc == DC - 1),
                                             inc=(s_pe, 1) if dc == DC - 1 else None)
                            P.wait("scalar", t_gmm)
                            P.wait("scalar", sS_last[cpar])
                            t_silu = P.op("scalar", "activation",
                                          out=sS[:, cpar * NT: cpar * NT + tlen],
                                          in_=psA[:, cpar * 512: cpar * 512 + tlen],
                                          func=AF.Silu, inc=(s_act, 1))
                            silu_tok[cpar] = t_silu
                            P.waits("vector", t_silu, t_rwmm)
                            t_t1 = P.op("vector", "tensor_tensor",
                                        t1[:, cpar * NT: cpar * NT + tlen],
                                        sS[:, cpar * NT: cpar * NT + tlen],
                                        psC[:, cpar * 512: cpar * 512 + tlen],
                                        ALU.mult, inc=(s_dve, 1))
                            psC_last[cpar] = t_t1
                            sS_last[cpar] = t_t1
                            P.wait("vector", t_umm)
                            P.wait("vector", h_guard[par])
                            t_hw = P.op("vector", "tensor_tensor",
                                        h_sb[:, gbase_h + ci * NT:
                                             gbase_h + ci * NT + tlen],
                                        t1[:, cpar * NT: cpar * NT + tlen],
                                        psB[:, cpar * 512: cpar * 512 + tlen],
                                        ALU.mult, inc=(s_dve, 1))
                            h_tok[cpar] = t_hw
                            last_h = t_hw
                        P.wait("tensor", last_h)
                        for mm in range(8):
                            dpar = mm % 2
                            P.wait("tensor", flush_tok[dpar])
                            for ci in range(ncg):
                                tmm = P.op("tensor", "matmul",
                                           psD[:, dpar * 512: dpar * 512 + tlen],
                                           wd_b[:, gbase_d + ci * D + mm * 128:
                                                gbase_d + ci * D + (mm + 1) * 128],
                                           h_sb[:, gbase_h + ci * NT:
                                                gbase_h + ci * NT + tlen],
                                           start=(ci == 0), stop=(ci == ncg - 1),
                                           inc=(s_pe, 1)
                                           if ci == ncg - 1 else None)
                            P.wait("vector", tmm)
                            xs = x_sb[:, mm * T + tt0: mm * T + tt0 + tlen]
                            t_fl = P.op("vector", "tensor_tensor", xs, xs,
                                        psD[:, dpar * 512: dpar * 512 + tlen],
                                        ALU.add, inc=(s_dve, 1))
                            flush_tok[dpar] = t_fl
                        h_guard[par] = tmm
                        if pending:
                            gi2, pc = pending.pop(0)
                            emit_group_piece(gi2, pc, tmm)
                    while pending:
                        gi2, pc = pending.pop(0)
                        emit_group_piece(gi2, pc, h_guard[par])
                    if gi + 2 < len(cfg.groups):
                        free_tok[par] = h_guard[par]
                        pending = [(gi + 2, 0), (gi + 2, 1), (gi + 2, 2)]
                t_done = t_fl
                if call == 1:
                    for nm_, tile_, dt_ in (("ff_wg", wg_b, BF16),
                                            ("ff_wu", wu_b, BF16),
                                            ("ff_wd", wd_b, BF16),
                                            ("ff_m1", m1_b, BF16),
                                            ("ff_h", h_sb, BF16),
                                            ("ff_sS", sS, BF16),
                                            ("ff_t1", t1, F32)):
                        d_ = dbg(nm_, list(tile_[:].shape), dt_)
                        if d_ is not None:
                            P.wait("sync", t_done)
                            P.op("sync", "dma_start", out=d_[:, :],
                                 in_=tile_[:], inc=(s_st, 16))
                d = dbg(f"x_after_{call}", [128, DC * T])
                if d is not None:
                    P.wait("sync", t_done)
                    P.op("sync", "dma_start", out=d[:, :], in_=x_sb[:],
                         inc=(s_st, 16))
                return t_done

        def emit_attn(t_prev):
            with ExitStack() as asx:
                asb = lambda n, s, dt=F32: asx.enter_context(
                    nc.sbuf_tensor(n, s, dt))
                wk_sb = asb("wk_sb", [128, DC * D], BF16)
                wv_sb = asb("wv_sb", [128, DC * D], BF16)
                wr_sb = asb("wr_sb", [128, DC * D], BF16)
                wgg_sb = asb("wgg_sb", [128, DC * D], BF16)
                wo_sb = asb("wo_sb", [128, DC * D], BF16)
                PThL = asb("PThL", [128, DC * 128], BF16)
                PTtL = asb("PTtL", [128, DC * 128], BF16)
                PTtB = asb("PTtB", [128, DC * 64], BF16)
                PTq = asb("PTq", [128, DC * 128], BF16)
                tf_full = asb("tf_full", [128, DC * 128], BF16)
                REVP = asb("REVP", [128, D])
                WCf = asb("WCf", [128, 512])
                IL = asb("IL", [128, 64])
                IB96 = asb("IB96", [128, 64])
                Iq = asb("Iq", [128, 128])
                revrow = asb("revrow", [1, 128])
                e1 = asb("e1", [128, 8])
                ptscr = asb("ptscr", [128, 128])
                S_true = asb("S_true", [128, 512])
                S_bf = asb("S_bf", [128, 512], BF16)
                xx = asb("xx", [128, DC * 128], BF16)
                Kfm = asb("Kfm", [128, DC * 128], BF16)
                Vfm = asb("Vfm", [128, DC * 128], BF16)
                Rfm = asb("Rfm", [128, DC * 128], BF16)
                Gfm = asb("Gfm", [128, DC * 128], BF16)
                Ktm = asb("Ktm", [128, D], BF16)
                Vtm = asb("Vtm", [128, D], BF16)
                k2 = asb("k2", [128, D], BF16)
                khatL = asb("khatL", [128, DC * 128], BF16)
                ktilL = asb("ktilL", [128, DC * 128], BF16)
                ktilB = asb("ktilB", [128, DC * 64], BF16)
                qt = asb("qt", [128, DC * 128], BF16)
                Am = asb("Am", [128, 2 * 128], BF16)
                w3 = asb("w3", [128, DC * 128], BF16)
                lx2 = asb("alx2", [128, 256])
                lrows = asb("alrows", [1, 512])
                ltscr = asb("altscr", [128, 128])

                psP = asx.enter_context(nc.psum_tensor("apsP", [128, 1024], F32))
                psS = asx.enter_context(nc.psum_tensor("apsS", [128, 512], F32))
                psAp = asx.enter_context(nc.psum_tensor("apsAp", [128, 1024], F32))
                psO = asx.enter_context(nc.psum_tensor("apsO", [128, 1024], F32))

                P.wait("gpsimd", t_prev)
                aw_sems = (s_wg[0], s_wu[0], s_wd[0], s_wg[1], s_wu[1])
                aw_toks = []
                for (nm, tile), s_aw in zip(
                        (("wkT", wk_sb), ("wvT", wv_sb), ("wrT", wr_sb),
                         ("wgT", wgg_sb), ("woT", wo_sb)), aw_sems):
                    aw_toks.append(P.op(
                        "gpsimd", "dma_start",
                        out=tile[:].rearrange("p (c e) -> p c e", e=D),
                        in_=ap[nm].rearrange("(c p) e -> p c e", p=128),
                        inc=(s_aw, 16)))

                P.op("gpsimd", "iota", Iq[:], pattern=[[1, 128]], base=1,
                     channel_multiplier=0, allow_small_or_imprecise_dtypes=True)
                P.op("gpsimd", "iota", IL[:], pattern=[[1, 64]], base=-32,
                     channel_multiplier=0, allow_small_or_imprecise_dtypes=True)
                P.op("gpsimd", "iota", IB96[:], pattern=[[-1, 64]], base=96,
                     channel_multiplier=0, allow_small_or_imprecise_dtypes=True)
                t_iq = P.op("gpsimd", "iota", revrow[:], pattern=[[-1, 128]],
                            base=127, channel_multiplier=0,
                            allow_small_or_imprecise_dtypes=True, inc=(s_gp, 1))
                P.waits("vector", t_iq, t_small)
                t_ptlast = None
                for ec in range(DC):
                    P.wait("vector", t_ptlast)
                    LN2x30 = 20.794415416798359
                    t_m = P.op("vector", "tensor_scalar", ptscr[:], Iq[:],
                               td_s[:, ec:ec + 1], None, ALU.mult)
                    t_m = P.op("vector", "tensor_scalar", ptscr[:], ptscr[:],
                               2 * LN2x30, None, ALU.add, inc=(s_dve, 1))
                    P.wait("scalar", t_m)
                    t_q = P.op("scalar", "activation", out=PTq[:, ts(ec, 128)],
                               in_=ptscr[:], func=AF.Exp, scale=-1.0,
                               inc=(s_act, 1))
                    P.wait("vector", t_q)
                    t_m = P.op("vector", "tensor_scalar", ptscr[:, 0:64],
                               IB96[:], td_s[:, ec:ec + 1], None, ALU.mult)
                    t_m = P.op("vector", "tensor_scalar", ptscr[:, 0:64],
                               ptscr[:, 0:64], LN2x30, None, ALU.add,
                               inc=(s_dve, 1))
                    P.wait("scalar", t_m)
                    t_b = P.op("scalar", "activation", out=PTtB[:, ts(ec, 64)],
                               in_=ptscr[:, 0:64], func=AF.Exp, scale=-1.0,
                               inc=(s_act, 1))
                    P.wait("vector", t_b)
                    P.op("vector", "tensor_scalar", ptscr[:, 0:64], IL[:],
                         td_s[:, ec:ec + 1], None, ALU.mult)
                    P.op("vector", "tensor_scalar", ptscr[:, 0:64],
                         ptscr[:, 0:64], -87.0, None, ALU.max)
                    P.op("vector", "tensor_scalar", ptscr[:, 0:64],
                         ptscr[:, 0:64], 87.0, None, ALU.min)
                    P.op("vector", "tensor_scalar", ptscr[:, 64:128],
                         ptscr[:, 0:64], LN2x30, None, ALU.add)
                    t_m = P.op("vector", "tensor_scalar", ptscr[:, 0:64],
                               ptscr[:, 0:64], -LN2x30, None, ALU.add,
                               inc=(s_dve, 1))
                    P.wait("scalar", t_m)
                    for hf in range(2):
                        P.op("scalar", "activation",
                             out=PThL[:, ec * 128 + hf * 64:
                                      ec * 128 + hf * 64 + 64],
                             in_=ptscr[:, 64:128], func=AF.Exp, scale=-1.0)
                        t_ptlast = P.op("scalar", "activation",
                                        out=PTtL[:, ec * 128 + hf * 64:
                                                 ec * 128 + hf * 64 + 64],
                                        in_=ptscr[:, 0:64], func=AF.Exp,
                                        scale=1.0, inc=(s_act, 1))
                t_tfm = P.op("gpsimd", "memset", tf_full[:], 1.0, inc=(s_gp, 1))
                P.wait("vector", t_tfm)
                for ec in range(DC):
                    t_tff = P.op("vector", "tensor_scalar",
                                 tf_full[:, ts(ec, 128)],
                                 tf_full[:, ts(ec, 128)],
                                 tf_s[:, ec:ec + 1], None, ALU.mult,
                                 inc=(s_dve, 1))
                P.wait("tensor", t_iq)
                P.wait("tensor", t_prev)
                t_rp = P.op("tensor", "matmul", psP[:, 0:512], revrow[:],
                            td_row[0:1, 0:512], start=True, stop=True)
                t_rp2 = P.op("tensor", "matmul", psP[:, 512:1024], revrow[:],
                             td_row[0:1, 512:1024], start=True, stop=True,
                             inc=(s_pe, 1))
                P.wait("scalar", t_rp2)
                P.op("scalar", "activation", out=REVP[:, 0:512],
                     in_=psP[:, 0:512], func=AF.Exp, scale=-1.0)
                t_revp = P.op("scalar", "activation", out=REVP[:, 512:1024],
                              in_=psP[:, 512:1024], func=AF.Exp, scale=-1.0,
                              inc=(s_act, 1))
                t_e1 = P.op("scalar", "activation", out=e1[:], in_=td_s[:],
                            func=AF.Exp, scale=-128.0, inc=(s_act, 1))
                P.wait("vector", t_e1)
                P.op("vector", "memset", WCf[:], 1.0)
                for p in range(8):
                    t_wc = P.op("vector", "tensor_scalar", WCf[:, ts(p, 64)],
                                WCf[:, ts(p, 64)], e1[:, p:p + 1], None,
                                ALU.mult, inc=(s_dve, 1))

                ln_scope = {"psR": psP, "psAB": psP, "x2": lx2, "rows": lrows,
                            "tscr": ltscr, "rstride": 128, "x2w": 128}

                for apar_ in range(2):
                    P.op("vector", "memset",
                         psAp[64:128, apar_ * 512: apar_ * 512 + 64], 0.0)

                t_Strue = None
                t_psS_read = None
                prev_epi = None
                prev_sel = [None, None]
                prev_am = [None, None]
                t_wo_fl = None
                ps_ev = {0: None, 1: None}
                t_vtm = None

                for tc in range(cfg.n_tc):
                    if tc == 0:
                        P.waits("tensor", t_prev, t_wc, t_revp, t_ptlast,
                                *aw_toks)
                        ln_scope["t_pe_pre"] = t_prev
                    else:
                        ln_scope["t_pe_pre"] = None
                    P.waits("tensor", ps_ev[0], ps_ev[1])
                    t_xx = emit_ln_apply(ln_scope, (tc * 128, 128), xx, 128, 0,
                                         True)
                    P.wait("tensor", t_xx)
                    for wt, dst in ((wk_sb, Ktm), (wv_sb, Vtm)):
                        for half in range(2):
                            P.wait("tensor", ps_ev[half])
                            for dc in range(DC):
                                tmm = P.op("tensor", "matmul",
                                           psP[:, half * 512:(half + 1) * 512],
                                           xx[:, ts(dc, 128)],
                                           wt[:, dc * D + half * 512:
                                              dc * D + (half + 1) * 512],
                                           start=(dc == 0), stop=(dc == DC - 1),
                                           inc=(s_pe, 1) if dc == DC - 1 else None)
                            P.wait("scalar", tmm)
                            ps_ev[half] = P.op("scalar", "copy",
                                               dst[:, half * 512:(half + 1) * 512],
                                               psP[:, half * 512:(half + 1) * 512],
                                               inc=(s_act, 1))
                        if dst is Vtm:
                            t_vtm = ps_ev[1]
                    cnt = 0
                    for wt, dst, fn in ((wk_sb, Kfm, AF.Identity),
                                        (wv_sb, Vfm, AF.Identity),
                                        (wr_sb, Rfm, AF.Identity),
                                        (wgg_sb, Gfm, AF.Silu)):
                        for ec in range(DC):
                            half = cnt % 2
                            cnt += 1
                            P.wait("tensor", ps_ev[half])
                            for dc in range(DC):
                                tmm = P.op("tensor", "matmul",
                                           psP[:, half * 512: half * 512 + 128],
                                           wt[:, dc * D + ec * 128:
                                              dc * D + (ec + 1) * 128],
                                           xx[:, ts(dc, 128)],
                                           start=(dc == 0), stop=(dc == DC - 1),
                                           inc=(s_pe, 1) if dc == DC - 1 else None)
                            P.wait("scalar", tmm)
                            ps_ev[half] = P.op("scalar", "activation",
                                               out=dst[:, ts(ec, 128)],
                                               in_=psP[:, half * 512:
                                                       half * 512 + 128],
                                               func=fn, inc=(s_act, 1))
                    t_proj_done = ps_ev[(cnt - 1) % 2]
                    t_proj_done2 = ps_ev[cnt % 2]
                    P.waits("vector", t_proj_done, t_proj_done2)
                    t_k2 = P.op("vector", "tensor_tensor", k2[:], Ktm[:],
                                REVP[:], ALU.mult, inc=(s_dve, 1))
                    P.waits("tensor", t_k2, t_vtm, t_psS_read)
                    for h in range(H):
                        base = (h % 2) * 64
                        tmm = P.op("tensor", "matmul",
                                   psS[base:base + 64, ts(h // 2, 64)],
                                   k2[:, h * HD:(h + 1) * HD],
                                   Vtm[:, h * HD:(h + 1) * HD],
                                   start=True, stop=True,
                                   inc=(s_pe, 1) if h == H - 1 else None)
                    t_sadd = tmm
                    if tc == 0:
                        P.wait("vector", t_sadd)
                        P.op("vector", "tensor_copy", S_true[:], psS[:])
                        t_Strue = P.op("vector", "tensor_copy", S_bf[:],
                                       S_true[:], inc=(s_dve, 1))
                        t_psS_read = t_Strue
                        continue
                    kv3 = lambda t_: t_[:].rearrange("q (c t) -> q c t", t=128)
                    P.op("vector", "tensor_tensor", khatL[:], Kfm[:],
                         PThL[:], ALU.mult)
                    P.op("vector", "tensor_tensor", ktilL[:], Kfm[:],
                         PTtL[:], ALU.mult)
                    P.op("vector", "tensor_tensor",
                         ktilB[:].rearrange("q (c t) -> q c t", t=64),
                         kv3(Kfm)[:, :, 0:64],
                         PTtB[:].rearrange("q (c t) -> q c t", t=64), ALU.mult)
                    t_con = P.op("vector", "tensor_tensor", qt[:], Kfm[:],
                                 PTq[:], ALU.mult, inc=(s_dve, 1))
                    P.waits("tensor", t_con, prev_epi, t_Strue)
                    t_sel = dict()
                    am_rd = dict()
                    for h in range(H):
                        base = (h % 2) * 64
                        ec = h // 2
                        apar = h % 2
                        c0 = apar * 512
                        P.wait("tensor", t_sel.get(h - 2, prev_sel[apar]))
                        P.op("tensor", "matmul",
                             psAp[0:64, c0:c0 + 64],
                             ktilL[base:base + 64, ec * 128: ec * 128 + 64],
                             khatL[base:base + 64, ec * 128: ec * 128 + 64],
                             start=True, stop=True)
                        P.op("tensor", "matmul",
                             psAp[0:64, c0 + 64:c0 + 128],
                             ktilB[base:base + 64, ts(ec, 64)],
                             khatL[base:base + 64,
                                   ec * 128 + 64: ec * 128 + 128],
                             start=True, stop=True)
                        t_amm = P.op("tensor", "matmul",
                                     psAp[64:128, c0 + 64:c0 + 128],
                                     ktilL[base:base + 64,
                                           ec * 128 + 64: ec * 128 + 128],
                                     khatL[base:base + 64,
                                           ec * 128 + 64: ec * 128 + 128],
                                     start=True, stop=True, inc=(s_pe, 1))
                        P.wait("vector", t_amm)
                        P.wait("vector", am_rd.get(h - 2, prev_am[apar]))
                        t_sel[h] = P.op("vector", "select", Am[:, ts(apar, 128)],
                                        tri_u8[:],
                                        psAp[:, apar * 512: apar * 512 + 128],
                                        zeros_bf[:], inc=(s_dve, 1))
                        P.wait("tensor", t_sel[h])
                        P.op("tensor", "matmul",
                             psO[base:base + 64, ts(ec, 128)],
                             Vtm[:, h * HD:(h + 1) * HD], Am[:, ts(apar, 128)],
                             start=True, stop=False)
                        t_out = P.op("tensor", "matmul",
                                     psO[base:base + 64, ts(ec, 128)],
                                     S_bf[base:base + 64, ts(ec, 64)],
                                     qt[base:base + 64, ts(ec, 128)],
                                     start=False, stop=True, inc=(s_pe, 1))
                        am_rd[h] = t_out
                    prev_sel = [t_sel[H - 2], t_sel[H - 1]]
                    prev_am = [am_rd[H - 2], am_rd[H - 1]]
                    t_outs = t_out
                    P.wait("vector", t_outs)
                    P.op("vector", "tensor_tensor", S_true[:], S_true[:],
                         WCf[:], ALU.mult)
                    t_Strue = P.op("vector", "tensor_tensor", S_true[:],
                                   S_true[:], psS[:], ALU.add, inc=(s_dve, 1))
                    t_psS_read = t_Strue
                    t_Strue = P.op("vector", "tensor_copy", S_bf[:], S_true[:],
                                   inc=(s_dve, 1))
                    P.op("vector", "tensor_scalar", psO[:, 0:1024],
                         psO[:, 0:1024], 2.0 ** 60, None, ALU.mult)
                    P.op("vector", "tensor_tensor", xx[:], Kfm[:], Vfm[:],
                         ALU.mult)
                    P.op("vector", "tensor_tensor", xx[:], xx[:], tf_full[:],
                         ALU.mult)
                    P.op("vector", "tensor_tensor", psO[:, 0:1024],
                         psO[:, 0:1024], xx[:], ALU.add)
                    P.op("vector", "tensor_tensor", psO[:, 0:1024],
                         psO[:, 0:1024], Rfm[:], ALU.mult)
                    t_w3 = P.op("vector", "tensor_tensor", w3[:],
                                psO[:, 0:1024], Gfm[:], ALU.mult,
                                inc=(s_dve, 1))
                    if "at1" in cfg.debug and tc == 1:
                        toks = []
                        P.waits("sync", t_w3, t_Strue)
                        for nm_, tile_, dt_ in (
                                ("at_xx", xx, BF16), ("at_Ktm", Ktm, BF16),
                                ("at_Vtm", Vtm, BF16), ("at_Kfm", Kfm, BF16),
                                ("at_Vfm", Vfm, BF16), ("at_Rfm", Rfm, BF16),
                                ("at_Gfm", Gfm, BF16), ("at_khat", khatL, BF16),
                                ("at_ktil", ktilL, BF16), ("at_qt", qt, BF16),
                                ("at_w3", w3, BF16), ("at_Sbf", S_bf, BF16),
                                ("at_k2", k2, BF16), ("at_Am", Am, BF16),
                                ("at_PTh", PThL, F32), ("at_REVP", REVP, F32),
                                ("at_WCf", WCf, F32)):
                            d_ = dbg(nm_, list(tile_[:].shape), dt_)
                            if d_ is not None:
                                tk = P.op("sync", "dma_start", out=d_[:, :],
                                          in_=tile_[:], inc=(s_st, 16))
                                toks.append(tk)
                        if toks:
                            P.wait("vector", toks[-1])
                            P.wait("tensor", toks[-1])
                    P.wait("tensor", t_w3)
                    for eo in range(DC):
                        for dc in range(DC):
                            tmm = P.op("tensor", "matmul",
                                       psO[:, eo * 128:(eo + 1) * 128],
                                       wo_sb[:, dc * D + eo * 128:
                                             dc * D + (eo + 1) * 128],
                                       w3[:, ts(dc, 128)],
                                       start=(dc == 0), stop=(dc == DC - 1),
                                       inc=(s_pe, 1)
                                       if (eo == DC - 1 and dc == DC - 1)
                                       else None)
                    P.wait("vector", tmm)
                    xv = x_sb[:].rearrange("q (c t) -> q c t", t=T)[
                        :, :, tc * 128:(tc + 1) * 128]
                    t_wo_fl = P.op("vector", "tensor_tensor", xv, xv,
                                   psO[:, 0:1024].rearrange(
                                       "q (c t) -> q c t", t=128),
                                   ALU.add, inc=(s_dve, 1))
                    prev_epi = t_wo_fl
                d = dbg("x_after_2", [128, DC * T])
                if d is not None:
                    P.wait("sync", t_wo_fl)
                    P.op("sync", "dma_start", out=d[:, :], in_=x_sb[:],
                         inc=(s_st, 16))
                return t_wo_fl

        tdone = None
        if "1" in cfg.phases:
            tdone = emit_ffn(1, tdone)
            P.flush(block)
        if "2" in cfg.phases:
            tdone = emit_attn(tdone)
            P.flush(block)
        if "3" in cfg.phases:
            tdone = emit_ffn(3, tdone)
        P.wait("sync", tdone)
        P.op("sync", "dma_start",
             out=ap["y"].rearrange("(c p) t -> p c t", p=128),
             in_=x_sb[:].rearrange("p (c t) -> p c t", t=T)[:, :, W:T],
             inc=(s_st, 16))
        P.wait("sync", (s_st, s_st.count))
        P.flush(block)

    return nc, list(dbg_outs)



def prep_host(cfg: Cfg, W_attn, ffn, td, tf, tm, gate_w, gate_bias):
    import ml_dtypes
    f = np.ascontiguousarray
    Wg_e, Wu_e, Wd_e, Sg, Su, Sd = [np.asarray(a, np.float32) for a in ffn]
    D = cfg.D
    gateT = f(np.concatenate([Wg_e.transpose(2, 0, 1).reshape(D, -1), Sg.T], axis=1))
    upT = f(np.concatenate([Wu_e.transpose(2, 0, 1).reshape(D, -1), Su.T], axis=1))
    downT = f(np.concatenate([Wd_e.transpose(0, 2, 1).reshape(-1, D), Sd.T], axis=0))
    assert gateT.shape == (D, cfg.NI) and downT.shape == (cfg.NI, D)
    eid = np.empty(cfg.NI, np.int64)
    eid[:cfg.NE * cfg.I_exp] = np.arange(cfg.NE * cfg.I_exp) // cfg.I_exp
    eid[cfg.NE * cfg.I_exp:] = cfg.NE
    m1T = np.zeros((cfg.NE + 1, cfg.NI), np.float32)
    m1T[eid, np.arange(cfg.NI)] = 1.0
    m1T = f(m1T.astype(ml_dtypes.bfloat16))

    td = np.asarray(td, np.float32).reshape(cfg.H, cfg.HD)
    tf = np.asarray(tf, np.float32).reshape(cfg.H, cfg.HD)
    tm = np.asarray(tm, np.float32).reshape(-1)
    td_s = f(td.reshape(cfg.H // 2, 2, cfg.HD).transpose(1, 2, 0).reshape(128, -1))
    tf_s = f(tf.reshape(cfg.H // 2, 2, cfg.HD).transpose(1, 2, 0).reshape(128, -1))
    tm_s = f(tm.reshape(cfg.DC, 128).T)
    Wk, Wv, Wr, Wgt, Wo = [np.asarray(a, np.float32) for a in W_attn]
    return {
        "gateT": gateT, "upT": upT, "downT": downT, "m1T": m1T,
        "wkT": f(Wk.T), "wvT": f(Wv.T), "wrT": f(Wr.T), "wgT": f(Wgt.T),
        "woT": f(Wo.T),
        "gwT": f(np.asarray(gate_w, np.float32).T),
        "td_s": td_s, "tf_s": tf_s, "tm_s": tm_s,
        "td_row": f(td.reshape(1, -1)),
        "gb_row": f(np.asarray(gate_bias, np.float32).reshape(1, -1)),
    }


def run_cores(cfg: Cfg, xT_full, host, **kw):
    nc, dbg_names = build_program(cfg)
    in_maps = []
    for c in range(cfg.n_cores):
        t0 = c * cfg.T_main
        halo = (np.zeros((cfg.D, cfg.W), np.float32) if c == 0
                else np.ascontiguousarray(xT_full[:, t0 - cfg.W:t0]))
        xc = np.concatenate([halo, xT_full[:, t0:t0 + cfg.T_main]], axis=1)
        m = {"xT": np.ascontiguousarray(xc)}
        m.update(host)
        in_maps.append(m)
    res = run_bass_kernel_spmd(nc, in_maps, core_ids=list(range(cfg.n_cores)),
                               **kw)
    return res


def kernel(**inputs):
    cfg = Cfg()
    x = np.asarray(inputs["L0_attn_x"], np.float32)
    Ttot = x.shape[1]
    assert Ttot == cfg.T_main * cfg.n_cores
    xT = np.ascontiguousarray(x[0].T)
    host = prep_host(
        cfg,
        (inputs["L0_attn_Wk"], inputs["L0_attn_Wv"], inputs["L0_attn_Wr"],
         inputs["L0_attn_Wg"], inputs["L0_attn_Wo"]),
        (inputs["L0_ffn_Wg"], inputs["L0_ffn_Wu"], inputs["L0_ffn_Wd"],
         inputs["L0_ffn_Sg"], inputs["L0_ffn_Su"], inputs["L0_ffn_Sd"]),
        inputs["L0_attn_time_decay"], inputs["L0_attn_time_first"],
        inputs["L0_attn_time_mix"], inputs["L0_ffn_gate_w"],
        inputs["L0_ffn_gate_bias"])
    res = run_cores(cfg, xT, host)
    y = np.concatenate([res.results[c]["y"] for c in range(cfg.n_cores)], axis=1)
    return np.ascontiguousarray(y.T).reshape(1, Ttot, cfg.D).astype(np.float32)


if __name__ == "__main__":
    cfg = Cfg()
    nc, _ = build_program(cfg)
    print("instructions:", sum(1 for _ in nc.all_instructions()))
